# version 08
# baseline (speedup 1.0000x reference)
import numpy as np
import ml_dtypes
import concourse.bass as bass
import concourse.mybir as mybir
from concourse.bass_utils import run_bass_kernel_spmd

F32 = mybir.dt.float32
BF16 = mybir.dt.bfloat16
AF = mybir.ActivationFunctionType
ALU = mybir.AluOpType
AX = mybir.AxisListType

B, T, E = 8, 2048, 1024
NCHUNK = E // 128
ST_T = 512
NSTRIPE = T // ST_T
CC_N = ST_T // 128

GROUPS = [[0, 1, 2, 3]]

ENGS = ("sp", "pool", "te", "act", "dve")


class Tracker:
    N_LANES = {"pool": 8, "sp": 4, "te": 1, "act": 2, "dve": 1}

    def __init__(self, nc):
        self.nc = nc
        self.sem = {}
        self.cnt = {}
        for e in ENGS:
            self.sem[e] = nc.alloc_semaphore(name=f"tk_{e}")
            self.cnt[e] = 0
            for l in range(self.N_LANES[e]):
                n = f"{e}_d{l}"
                self.sem[n] = nc.alloc_semaphore(name=f"tk_{n}")
                self.cnt[n] = 0
        self.lane_i = {e: 0 for e in ENGS}
        self._own_waited = {}
        self.clock = {e: {s: 0 for s in self.sem} for e in ENGS}
        self.snap = {}
        self.w = {}
        self.r = {}
        self.prog = {e: [] for e in ENGS}
        self.nops = 0

    STRICT_SAME = ("dve", "act", "pool")

    def _collect(self, eng, reads, writes, self_dma, strict=False):
        need = {}
        strict = strict or eng in self.STRICT_SAME

        def add(tok):
            pe, ps, pv, pdma = tok
            if pe == eng and not pdma and not strict:
                return
            if need.get(ps, 0) < pv:
                need[ps] = pv

        for b in reads:
            lw = self.w.get(b)
            if lw is not None:
                add(lw)
        for b in writes:
            lw = self.w.get(b)
            if lw is not None:
                add(lw)
            for (pe, ps), (pv, pdma) in self.r.get(b, {}).items():
                if pe == eng and not pdma and not strict:
                    continue
                if need.get(ps, 0) < pv:
                    need[ps] = pv
        return need

    def _apply_waits(self, eng, need):
        waits = []
        ck = self.clock[eng]
        for s, v in sorted(need.items()):
            if s == eng:
                if v > self._own_waited.get(eng, 0):
                    waits.append((s, v))
                    self._own_waited[eng] = v
                continue
            if ck[s] >= v:
                continue
            waits.append((s, v))
            snap = self.snap.get((s, v))
            if snap:
                for s2, v2 in snap.items():
                    if ck[s2] < v2:
                        ck[s2] = v2
            if ck[s] < v:
                ck[s] = v
        return waits

    def op(self, eng, fn, reads=(), writes=(), dma=False, strict=False):
        need = self._collect(eng, reads, writes, dma, strict)
        attach = None
        if dma:
            lane = f"{eng}_d{self.lane_i[eng] % self.N_LANES[eng]}"
            self.lane_i[eng] += 1
            need.pop(lane, None)
            attach = (lane, self.cnt[lane])
            ck = self.clock[eng]
            if ck[lane] < self.cnt[lane]:
                ck[lane] = self.cnt[lane]
            waits = self._apply_waits(eng, need)
            self.cnt[lane] += 16
            v_after = self.cnt[lane]
            sem_used = lane
        else:
            waits = self._apply_waits(eng, need)
            self.cnt[eng] += 1
            v_after = self.cnt[eng]
            sem_used = eng
        if not dma:
            self.clock[eng][sem_used] = v_after
        snap = dict(self.clock[eng])
        snap[sem_used] = max(snap.get(sem_used, 0), v_after)
        self.snap[(sem_used, v_after)] = snap
        self.prog[eng].append((waits, fn, sem_used, 16 if dma else 1, attach))
        self.nops += 1
        tok = (eng, sem_used, v_after, dma)
        for b in writes:
            self.w[b] = tok
            self.r[b] = {}
        for b in reads:
            self.r.setdefault(b, {})[(eng, sem_used)] = (v_after, dma)
        return tok

    def barrier(self, engines=ENGS):
        totals = dict(self.cnt)
        for e in engines:
            need = {}
            for s, v in totals.items():
                if v > 0 and s != e:
                    need[s] = v
            waits = self._apply_waits(e, need)
            if waits:
                self.prog[e].append((waits, None, e, 0, None))

    def finalize(self):
        nc = self.nc
        secname = {"sp": "sync", "pool": "gpsimd", "te": "tensor", "act": "scalar", "dve": "vector"}
        with nc.Block() as blk:
            for e in ENGS:
                entries = self.prog[e]
                sems = self.sem

                def make(entries=entries):
                    def run(eng):
                        for waits, fn, sname, inc, attach in entries:
                            for (p, c) in waits:
                                eng.wait_ge(sems[p], c)
                            if fn is not None:
                                ins = fn(eng)
                                if attach is not None and attach[1] > 0:
                                    ins._wait_ge(sems[attach[0]], attach[1])
                                ins.then_inc(sems[sname], inc)
                    return run
                getattr(blk, secname[e])(make())


class KB:

    def __init__(self, blocks):
        nc = bass.Bass("TRN2", target_bir_lowering=False)
        self.nc = nc
        self.blocks = list(blocks)
        self.tk = Tracker(nc)
        self.sbuf_used = 0
        self.tensors = {}
        self.dram = {}
        self.in_names = []
        self.pp = [nc.alloc_psum_tensor(f"pp{i}", [128, 512], F32) for i in range(6)]
        self.pt = [nc.alloc_psum_tensor(f"pt{i}", [128, 1024], BF16) for i in range(2)]
        self.ppk = [f"pp{i}" for i in range(6)]
        self.ptk = [f"pt{i}" for i in range(2)]
        self._act_set = None
        self._uid = 0

    def sb(self, name, shape, dt):
        t = self.nc.alloc_sbuf_tensor("sb_" + name, list(shape), dt)
        per = int(np.prod(shape[1:])) * mybir.dt.size(dt)
        self.sbuf_used += (per + 31) // 32 * 32
        self.tensors[name] = t
        return t

    def init_arena(self):
        self.sb("lnsq", [128, 512], BF16)
        self.sb("lnrstd", [128, 512], F32)
        self.ARENA_B = self.nc.sbuf_bytes_remaining - 64
        self.arena_backing = self.nc.alloc_sbuf_tensor(
            "sb_arena", [128, self.ARENA_B], mybir.dt.uint8)
        self.sbuf_used += self.ARENA_B
        self.arena_base = self.nc.lookup_mloc(self.arena_backing).addr
        self.arena_off = 0
        self.phase = 0

    def sba(self, name, shape, dt):
        per = int(np.prod(shape[1:])) * mybir.dt.size(dt)
        per = (per + 31) // 32 * 32
        assert self.arena_off + per <= self.ARENA_B,             f"arena overflow at {name}: {self.arena_off + per}"
        t = self.nc.alloc_sbuf_tensor_at(
            f"sb_p{self.phase}_{name}", list(shape), dt,
            offset=self.arena_base + self.arena_off)
        self.arena_off += per
        return t

    def arena_reset(self):
        self.tk.barrier()
        self.arena_off = 0
        self.phase += 1

    def din(self, name, shape, dt=F32):
        h = self.nc.dram_tensor(name, list(shape), dt, kind="ExternalInput")
        self.dram[name] = h
        self.in_names.append(name)
        return h.ap()

    def dout(self, name, shape, dt=F32):
        h = self.nc.dram_tensor(name, list(shape), dt, kind="ExternalOutput")
        self.dram[name] = h
        return h.ap()

    def uid(self):
        self._uid += 1
        return self._uid

    debug = False
    paranoid = 0

    def dbg(self, name, ap, reads):
        if not self.debug:
            return
        shape = list(ap.shape)
        o = self.dout("dbg_" + name, shape, ap.dtype)
        self.dma("sp", o, ap, reads=tuple(reads), writes=(("dbg", name),))

    def dma(self, q, out_ap, in_ap, reads=(), writes=()):
        self.tk.op(q, lambda e, o=out_ap, i=in_ap: e.dma_start(out=o, in_=i),
                   reads=reads, writes=writes, dma=True)

    def mm(self, ps_ap, lhsT, rhs, start, stop, reads, pskey, tile_position=None):
        self.tk.op("te",
                   lambda e, o=ps_ap, l=lhsT, r=rhs, s=start, t=stop, tp=tile_position:
                   e.matmul(o, l, r, start=s, stop=t, tile_position=tp),
                   reads=tuple(reads), writes=(pskey,))

    def tr(self, ps_ap, in_ap, reads, pskey):
        idb = self.tensors["c_idbf"]
        self.tk.op("te", lambda e, o=ps_ap, i=in_ap, d=idb: e.transpose(o, i, d[:, :]),
                   reads=tuple(reads) + ("c_idbf",), writes=(pskey,))

    def actv(self, out_ap, in_ap, func, reads, writes, bias=0.0, scale=1.0, accum_out=None):
        self.tk.op("act",
                   lambda e, o=out_ap, i=in_ap, f=func, b=bias, s=scale, a=accum_out:
                   e.activation(o, i, f, bias=b, scale=s, accum_out=a)
                   if a is not None else e.activation(o, i, f, bias=b, scale=s),
                   reads=tuple(reads), writes=tuple(writes))

    def v_tt(self, out_ap, in0, in1, op, reads, writes, eng="dve"):
        self.tk.op(eng, lambda e, o=out_ap, a=in0, b=in1, p=op: e.tensor_tensor(o, a, b, p),
                   reads=tuple(reads), writes=tuple(writes))

    def v_ts(self, out_ap, in0, s1, s2, op0, op1, reads, writes, eng="dve", accum_out=None):
        if op1 is None:
            self.tk.op(eng, lambda e, o=out_ap, i=in0, a=s1: e.tensor_scalar(o, i, a, None, op0),
                       reads=tuple(reads), writes=tuple(writes))
        else:
            self.tk.op(eng,
                       lambda e, o=out_ap, i=in0, a=s1, b=s2: e.tensor_scalar(o, i, a, b, op0, op1),
                       reads=tuple(reads), writes=tuple(writes))

    def v_stt(self, out_ap, in0, scalar, in1, op0, op1, reads, writes, eng="dve"):
        self.tk.op(eng,
                   lambda e, o=out_ap, a=in0, s=scalar, b=in1: e.scalar_tensor_tensor(o, a, s, b, op0, op1),
                   reads=tuple(reads), writes=tuple(writes))

    def v_copy(self, out_ap, in_ap, reads, writes, eng="dve"):
        self.tk.op(eng, lambda e, o=out_ap, i=in_ap: e.tensor_copy(o, i),
                   reads=tuple(reads), writes=tuple(writes))

    def v_scan(self, out_ap, d0, d1, init, op0, op1, reads, writes):
        self.tk.op("dve",
                   lambda e, o=out_ap, a=d0, b=d1, i=init: e.tensor_tensor_scan(o, a, b, i, op0, op1),
                   reads=tuple(reads), writes=tuple(writes))

    def v_red(self, out_ap, in_ap, reads, writes):
        self.tk.op("dve", lambda e, o=out_ap, i=in_ap: e.reduce_sum(o, i, AX.X),
                   reads=tuple(reads), writes=tuple(writes))

    def memset(self, ap, val, writes, eng="dve"):
        self.tk.op(eng, lambda e, o=ap, v=val: e.memset(o, v), reads=(), writes=tuple(writes))

    def setup_common(self):
        nc = self.nc
        for n, sh, dt in [("c_idbf", [128, 128], BF16), ("c_mask", [128, 128], BF16),
                          ("c_ones", [128, 128], BF16), ("c_onesrow", [1, 128], F32),
                          ("c_onesbf", [1, 128], BF16), ("c_pos", [128, 16], F32)]:
            self.din(n, sh, dt)
            t = self.sb(n, sh, dt)
            self.dma("sp", t[:, :], self.dram[n].ap(), writes=(n,))
        z = self.sb("c_zero", [128, 128], F32)
        self.memset(z[:, :], 0.0, writes=("c_zero",))
        e1 = self.sb("c_one", [128, 1], F32)
        self.memset(e1[:, :], 1.0, writes=("c_one",))
        self.x_in = self.din("x_in", [E, T], F32)
        self.x_out = self.dout("x_out", [E, T], F32)
        self.ST = [self.sb(f"st{c}", [128, T], F32) for c in range(NCHUNK)]
        self.XBF = [self.sb(f"xbf{c}", [128, T], BF16) for c in range(NCHUNK)]
        xr = self.x_in.rearrange("(c p) t -> c p t", p=128)
        for c in range(NCHUNK):
            self.dma("sp", self.ST[c][:, :], xr[c],
                     writes=tuple(("st", c, s) for s in range(NSTRIPE)))
        self.ring = [self.sb(f"ring{i}", [128, 4096], BF16) for i in range(4)]
        self.ring_i = 0
        if self.blocks[0] != 0:
            for c in range(NCHUNK):
                self.v_copy(self.XBF[c][:, :], self.ST[c][:, :],
                            reads=tuple(("st", c, s) for s in range(NSTRIPE)),
                            writes=tuple(("xbf", c, s) for s in range(NSTRIPE)))

    def store_out(self):
        for c in range(NCHUNK):
            xo = self.x_out.rearrange("(c p) t -> c p t", p=128)
            self.dma("sp", xo[c], self.ST[c][:, :],
                     reads=tuple(("st", c, s) for s in range(NSTRIPE)),
                     writes=(("out", c),))
        waits = [(f"sp_d{l}", self.tk.cnt[f"sp_d{l}"])
                 for l in range(self.tk.N_LANES["sp"]) if self.tk.cnt[f"sp_d{l}"] > 0]
        self.tk.prog["sp"].append((waits, None, "sp", 0, None))

    def wload(self, src_ap, cols, tag):
        i = self.ring_i % len(self.ring)
        self.ring_i += 1
        key = ("ring", i)
        tile = self.ring[i]
        self.dma("pool", tile[:, 0:cols], src_ap, reads=(), writes=(key,))
        return tile, key

    def ln_e(self, s, y_psum_evacs_done_keys=()):
        sl = slice(s * ST_T, (s + 1) * ST_T)
        ones = self.tensors["c_ones"]
        st_keys = tuple(("st", c, s) for c in range(NCHUNK))
        for c in range(NCHUNK):
            self.v_copy(self.XBF[c][:, sl], self.ST[c][:, sl],
                        reads=(("st", c, s),), writes=(("xbf", c, s),))
        sq = self.tensors["lnsq"]
        rstd = self.tensors["lnrstd"]
        for ss in range(ST_T // 512):
            s0 = s * ST_T + ss * 512
            ssl = slice(s0, s0 + 512)
            pm, pmk = self.pp[4], self.ppk[4]
            pv, pvk = self.pp[5], self.ppk[5]
            for c in range(NCHUNK):
                self.mm(pm[:, :], ones[:, :], self.XBF[c][:, ssl], c == 0, c == NCHUNK - 1,
                        ("c_ones", ("xbf", c, s)), pmk)
            for c in range(NCHUNK):
                self.v_tt(self.ST[c][:, ssl], self.ST[c][:, ssl], pm[:, :], ALU.subtract,
                          reads=(("st", c, s), pmk), writes=(("st", c, s),))
            for c in range(NCHUNK):
                self.actv(sq[:, :], self.ST[c][:, ssl], AF.Square,
                          reads=(("st", c, s),), writes=("lnsq",))
                self.mm(pv[:, :], ones[:, :], sq[:, :], c == 0, c == NCHUNK - 1,
                        ("c_ones", "lnsq"), pvk)
            self.actv(rstd[:, :], pv[:, :], AF.Ln, reads=(pvk, "c_eps5"), writes=("lnrstd",),
                      bias=self.tensors["c_eps5"][:, 0:1], scale=1.0)
            self.actv(rstd[:, :], rstd[:, :], AF.Exp, reads=("lnrstd",), writes=("lnrstd",),
                      scale=-0.5)
            for c in range(NCHUNK):
                self.v_tt(self.ST[c][:, ssl], self.ST[c][:, ssl], rstd[:, :], ALU.mult,
                          reads=(("st", c, s), "lnrstd"), writes=(("st", c, s),))
                self.v_copy(self.XBF[c][:, ssl], self.ST[c][:, ssl],
                            reads=(("st", c, s),), writes=(("xbf", c, s),))


def emit_l0_ffn(k: KB):
    nc = k.nc
    w12 = k.din("f0_w12", [32, 128, 2048])
    h1 = k.din("f0_h1", [32, 128, 4096])
    h2 = k.din("f0_h2", [32, 128, 4096])
    w3 = k.din("f0_w3", [8, 128, 4096])
    s12 = k.din("f0_s12", [128, 64])
    sh = k.din("f0_sh", [128, 64])
    s3 = k.din("f0_s3", [128, 8])
    s12t = k.sba("f0_s12t", [128, 64], F32)
    sht = k.sba("f0_sht", [128, 64], F32)
    s3t = k.sba("f0_s3t", [128, 8], F32)
    k.dma("sp", s12t[:, :], s12, writes=("f0_s12t",))
    k.dma("sp", sht[:, :], sh, writes=("f0_sht",))
    k.dma("sp", s3t[:, :], s3, writes=("f0_s3t",))

    HA = [k.sba(f"f0_ha{i}", [128, 512], BF16) for i in range(32)]
    HB = [k.sba(f"f0_hb{i}", [128, 512], BF16) for i in range(32)]

    for s in range(NSTRIPE):
        sl = slice(s * ST_T, (s + 1) * ST_T)
        xkeys = tuple(("xbf", c, s) for c in range(NCHUNK))
        for oc in range(32):
            wt, wk = k.wload(w12[oc], 2048, "w12")
            pa, pak = k.pp[oc % 2], k.ppk[oc % 2]
            pb, pbk = k.pp[2 + oc % 2], k.ppk[2 + oc % 2]
            for kc in range(NCHUNK):
                k.mm(pa[:, :], wt[:, kc * 128:(kc + 1) * 128], k.XBF[kc][:, sl],
                     kc == 0, kc == 7, (wk, ("xbf", kc, s)), pak)
            for kc in range(NCHUNK):
                k.mm(pb[:, :], wt[:, 1024 + kc * 128:1024 + (kc + 1) * 128], k.XBF[kc][:, sl],
                     kc == 0, kc == 7, (wk, ("xbf", kc, s)), pbk)
            k.actv(HA[oc][:, :], pa[:, :], AF.Silu, reads=(pak, "f0_s12t"),
                   writes=(("ha", oc),), scale=s12t[:, oc:oc + 1])
            k.actv(HB[oc][:, :], pb[:, :], AF.Copy, reads=(pbk, "f0_s12t"),
                   writes=(("hb", oc),), scale=s12t[:, 32 + oc:33 + oc])
            k.v_tt(HA[oc][:, :], HA[oc][:, :], HB[oc][:, :], ALU.mult,
                   reads=(("ha", oc), ("hb", oc)), writes=(("ha", oc),))
        for oc in range(32):
            wt, wk = k.wload(h1[oc], 4096, "h1")
            pa, pak = k.pp[oc % 3], k.ppk[oc % 3]
            for kc in range(32):
                k.mm(pa[:, :], wt[:, kc * 128:(kc + 1) * 128], HA[kc][:, :],
                     kc == 0, kc == 31, (wk, ("ha", kc)), pak)
            k.actv(HB[oc][:, :], pa[:, :], AF.Gelu, reads=(pak, "f0_sht"),
                   writes=(("hb", oc),), scale=sht[:, oc:oc + 1])
        for oc in range(32):
            wt, wk = k.wload(h2[oc], 4096, "h2")
            pa, pak = k.pp[oc % 3], k.ppk[oc % 3]
            for kc in range(32):
                k.mm(pa[:, :], wt[:, kc * 128:(kc + 1) * 128], HB[kc][:, :],
                     kc == 0, kc == 31, (wk, ("hb", kc)), pak)
            k.actv(HA[oc][:, :], pa[:, :], AF.Gelu, reads=(pak, "f0_sht"),
                   writes=(("ha", oc),), scale=sht[:, 32 + oc:33 + oc])
        for ec in range(NCHUNK):
            wt, wk = k.wload(w3[ec], 4096, "w3")
            pa, pak = k.pp[ec % 3], k.ppk[ec % 3]
            for kc in range(32):
                k.mm(pa[:, :], wt[:, kc * 128:(kc + 1) * 128], HA[kc][:, :],
                     kc == 0, kc == 31, (wk, ("ha", kc)), pak)
            k.v_stt(k.ST[ec][:, sl], pa[:, :], s3t[:, ec:ec + 1], k.ST[ec][:, sl],
                    ALU.mult, ALU.add,
                    reads=(pak, "f0_s3t", ("st", ec, s)), writes=(("st", ec, s),))
        k.ln_e(s)


def emit_l1_ffn(k: KB):
    mk = k.din("f1_mk", [4, 128, 1024])
    up = k.din("f1_up", [4, 128, 4096])
    cp = k.din("f1_cproj", [64, 128, 2048])

    MK = [k.sba(f"f1_mk{h}", [128, 1024], BF16) for h in range(4)]
    for h in range(4):
        k.dma("pool", MK[h][:, :], mk[h], writes=(("mk", h),))
    UPT = k.sba("f1_upt", [128, 2048], BF16)
    Y = [k.sba(f"f1_y{i}", [128, 512], BF16) for i in range(32)]
    HS = [k.sba(f"f1_hs{i}", [128, 512], BF16) for i in range(16)]

    for s in range(NSTRIPE):
        sl = slice(s * ST_T, (s + 1) * ST_T)
        for h in range(4):
            for c in range(NCHUNK):
                i = h * 8 + c
                pa, pak = k.pp[i % 2], k.ppk[i % 2]
                k.mm(pa[:, :], MK[h][:, c * 128:(c + 1) * 128], k.XBF[c][:, sl],
                     True, True, (("mk", h), ("xbf", c, s)), pak)
                k.v_copy(Y[i][:, :], pa[:, :], reads=(pak,), writes=(("y", i),))
        for grp in range(8):
            hh = grp // 2
            k.dma("pool", UPT[:, :], up[hh][:, (grp % 2) * 2048:(grp % 2) * 2048 + 2048],
                  writes=("f1_upt",))
            gtiles = []
            for l in range(4):
                p0 = grp * 8 + 2 * l
                wt, wk = k.wload(cp[p0:p0 + 2].transpose([1, 0, 2]), 4096, "cp")
                gtiles.append((wt, wk))
            for j in range(16):
                hc = grp * 16 + j
                rem = hc % 32
                cpr, msub = rem // 4, rem % 4
                i = hh * 8 + cpr
                upcol = (cpr % 4) * 512 + msub * 128
                pa, pak = k.pp[4 + j % 2], k.ppk[4 + j % 2]
                k.mm(pa[:, :], UPT[:, upcol: upcol + 128],
                     Y[i][:, :], True, True, ("f1_upt", ("y", i)), pak)
                k.actv(HS[j][:, :], pa[:, :], AF.Gelu, reads=(pak,), writes=(("hs", j),))
            for half in range(2):
                for j in range(16):
                    wt, wk = gtiles[j // 4]
                    jj = j % 4
                    base = (jj // 2) * 2048 + (jj % 2) * 1024
                    for ec in range(half * 4, half * 4 + 4):
                        pa, pak = k.pp[ec % 4], k.ppk[ec % 4]
                        k.mm(pa[:, :], wt[:, base + ec * 128: base + (ec + 1) * 128],
                             HS[j][:, :], j == 0, j == 15, (wk, ("hs", j)), pak)
                for ec in range(half * 4, half * 4 + 4):
                    pa, pak = k.pp[ec % 4], k.ppk[ec % 4]
                    k.v_tt(k.ST[ec][:, sl], k.ST[ec][:, sl], pa[:, :], ALU.add,
                           reads=(("st", ec, s), pak), writes=(("st", ec, s),))
        k.ln_e(s)


def emit_l1_attn(k: KB):
    qt = k.din("a1_q", [8, 128, 1024])
    kt = k.din("a1_k", [8, 128, 1024])
    vt = k.din("a1_v", [1024, 1024])
    pj = k.din("a1_proj", [8, 128, 1024])
    alb = k.din("a1_ab", [16, 2])
    ab = k.sba("a1_ab", [16, 2], F32)
    k.dma("sp", ab[:, :], alb, writes=("a1_ab",))
    scc = k.sba("a1_scc", [16, 2], F32)
    scr = k.sba("a1_scr", [1, 16], F32)
    tmp = k.sba("a1_tmp", [16, 2], F32)
    mx = k.sba("a1_mx", [16, 2], F32)
    one = k.tensors["c_one"]
    k.actv(tmp[:, :], ab[:, :], AF.Abs, reads=("a1_ab",), writes=("a1_tmp",))
    k.actv(tmp[:, :], tmp[:, :], AF.Exp, reads=("a1_tmp",), writes=("a1_tmp",), scale=-1.0)
    k.actv(tmp[:, :], tmp[:, :], AF.Ln, reads=("a1_tmp", "c_one"), writes=("a1_tmp",),
           bias=one[0:16, 0:1])
    k.v_ts(mx[:, :], ab[:, :], 0.0, None, ALU.max, None, reads=("a1_ab",), writes=("a1_mx",))
    for c2 in range(2):
        k.actv(scc[:, c2:c2 + 1], tmp[:, c2:c2 + 1], AF.Relu,
               reads=("a1_tmp", "a1_mx"), writes=("a1_scc",), bias=mx[:, c2:c2 + 1])
        k.v_ts(scc[:, c2:c2 + 1], scc[:, c2:c2 + 1], 1e-6, None, ALU.add, None,
               reads=("a1_scc",), writes=("a1_scc",))
    k.v_ts(scc[:, 0:1], scc[:, 0:1], scc[:, 1:2], 8.0, ALU.mult, ALU.mult,
           reads=("a1_scc",), writes=("a1_scc",))
    k.dbg("scc", scc[:, :], ("a1_scc",))
    scds = k.nc.dram_tensor("a1_scds", [1, 16], F32, kind="Internal")
    k.dma("pool", scds.ap(), scc[0:16, 0:1], reads=("a1_scc",), writes=("a1_scds",))
    scbc = k.sba("a1_scbc", [128, 16], F32)
    k.dma("pool", scbc[:, :], scds.ap().broadcast_to([128, 16]),
          reads=("a1_scds",), writes=("a1_scbc",))
    k.dbg("scbc", scbc[:, :], ("a1_scbc",))
    RCP = [k.sba(f"a1_rcp{h}", [128, 16], F32) for h in range(16)]
    pos = k.tensors["c_pos"]
    dena = k.sba("a1_dena", [128, 16], F32)
    denb = k.sba("a1_denb", [128, 16], F32)
    k.dbg("cpos", k.tensors["c_pos"][:, :], ("c_pos",))
    for h in range(16):
        k.v_ts(dena[:, :], pos[:, :], scbc[:, h:h + 1], None, ALU.mult, None,
               reads=("c_pos", "a1_scbc"), writes=("a1_dena",))
        k.v_ts(denb[:, :], dena[:, :], 1e-6, None, ALU.max, None,
               reads=("a1_dena",), writes=("a1_denb",))
        if h == 0:
            k.dbg("dena", dena[:, :], ("a1_dena",))
            k.dbg("denb", denb[:, :], ("a1_denb",))
        k.tk.op("dve", lambda e, o=RCP[h][:, :], i=denb[:, :]: e.reciprocal(o, i),
                reads=("a1_denb",), writes=(("rcp", h),), strict=True)

    SF = [k.sba(f"a1_sf{h}", [128, 64], F32) for h in range(16)]
    SB = [k.sba(f"a1_sb{h}", [128, 64], BF16) for h in range(16)]
    for h in range(16):
        k.memset(SF[h][:, :], 0.0, writes=(("sf", h),), eng="pool")
        k.memset(SB[h][:, :], 0.0, writes=(("sbp", h),), eng="pool")

    PHQ = [k.sba(f"a1_phq{c}", [128, 512], BF16) for c in range(NCHUNK)]
    PHK = [k.sba(f"a1_phk{c}", [128, 512], BF16) for c in range(NCHUNK)]
    VT = [k.sba(f"a1_vt{t}", [128, 1024], BF16) for t in range(CC_N)]
    OT = [k.sba(f"a1_ot{t}", [128, 1024], BF16) for t in range(CC_N)]
    KT2 = [k.sba(f"a1_kt2{i}", [128, 128], BF16) for i in range(2)]
    MS = [k.sba(f"a1_ms{i}", [128, 128], BF16) for i in range(2)]
    mask = k.tensors["c_mask"]

    for s in range(NSTRIPE):
        sl = slice(s * ST_T, (s + 1) * ST_T)
        xkeys = tuple(("xbf", c, s) for c in range(NCHUNK))
        for part, tiles, src in (("q", PHQ, qt), ("k", PHK, kt)):
            for oc in range(NCHUNK):
                wt, wk = k.wload(src[oc], 1024, "qk")
                pa, pak = k.pp[oc % 2], k.ppk[oc % 2]
                for kc in range(NCHUNK):
                    k.mm(pa[:, :], wt[:, kc * 128:(kc + 1) * 128], k.XBF[kc][:, sl],
                         kc == 0, kc == 7, (wk, ("xbf", kc, s)), pak)
                k.v_ts(tiles[oc][:, :], pa[:, :], 0.0, None, ALU.max, None,
                       reads=(pak,), writes=((part, oc),))
                k.v_ts(tiles[oc][:, :], tiles[oc][:, :], 1e-6, None, ALU.add, None,
                       reads=((part, oc),), writes=((part, oc),))
        vtiles = []
        for g4 in range(2):
            wt, wk = k.wload(vt[g4 * 512:(g4 + 1) * 512, :].rearrange("(a p) m -> a p m", p=128).transpose([1, 0, 2]),
                             4096, "v")
            vtiles.append((wt, wk))
        for half in range(2):
            hsl = slice(half * 512, half * 512 + 512)
            for kc in range(NCHUNK):
                wt, wk = vtiles[kc // 4]
                for tc in range(CC_N):
                    pa, pak = k.pp[tc], k.ppk[tc]
                    k.mm(pa[:, :], k.XBF[kc][:, s * ST_T + tc * 128: s * ST_T + (tc + 1) * 128],
                         wt[:, (kc % 4) * 1024 + half * 512:(kc % 4) * 1024 + half * 512 + 512],
                         kc == 0, kc == 7, (wk, ("xbf", kc, s)), pak)
            for tc in range(CC_N):
                k.v_copy(VT[tc][:, hsl], k.pp[tc][:, :], reads=(k.ppk[tc],), writes=(("vt", tc, half),))
        if s == 0:
            k.dbg("phq0", PHQ[0][:, :], (("q", 0),))
            k.dbg("phk0", PHK[0][:, :], (("k", 0),))
            k.dbg("vt0", VT[0][:, :], (("vt", 0, 0), ("vt", 0, 1)))
            k.dbg("rcp0", RCP[0][:, :], (("rcp", 0),))
        if k.paranoid >= 2:
            k.tk.barrier()
        for cc in range(CC_N):
            g = s * CC_N + cc
            csl = slice(s * ST_T + cc * 128, s * ST_T + (cc + 1) * 128)
            if k.paranoid >= 3:
                k.tk.barrier()
            for kc in range(NCHUNK):
                ptp, ptpk = k.pt[kc % 2], k.ptk[kc % 2]
                k.tr(ptp[:, 0:128], PHK[kc][:, cc * 128:(cc + 1) * 128],
                     (("k", kc),), ptpk)
                kt2 = KT2[kc % 2]
                k.v_copy(kt2[:, :], ptp[:, 0:128], reads=(ptpk,), writes=(("a1_kt2", kc % 2),))
                for half in range(2):
                    h = kc * 2 + half
                    hb = 64 * half
                    rot = h % 2
                    psc, psck = k.pp[4 + rot], k.ppk[4 + rot]
                    k.mm(psc[:, 0:128], PHK[kc][hb:hb + 64, cc * 128:(cc + 1) * 128],
                         PHQ[kc][hb:hb + 64, cc * 128:(cc + 1) * 128], True, True,
                         (("k", kc), ("q", kc)), psck)
                    msb = MS[rot]
                    k.v_tt(msb[:, :], psc[:, 0:128], mask[:, :], ALU.mult,
                           reads=(psck, "c_mask"), writes=(("ms", rot),))
                    po, pok = k.pp[0 + rot], k.ppk[0 + rot]
                    k.mm(po[:, 0:64], msb[:, :], VT[cc][:, h * 64:(h + 1) * 64], True, False,
                         (("ms", rot), ("vt", cc, h // 8)), pok)
                    k.mm(po[:, 0:64], PHQ[kc][:, cc * 128:(cc + 1) * 128],
                         SB[h][:, :], False, True, (("q", kc), ("sbp", h)), pok)
                    k.v_ts(OT[cc][:, h * 64:(h + 1) * 64], po[:, 0:64], RCP[h][:, g:g + 1],
                           None, ALU.mult, None, reads=(pok, ("rcp", h)), writes=(("ot", cc, h),))
                    pd, pdk = k.pp[2 + rot], k.ppk[2 + rot]
                    k.mm(pd[hb:hb + 64, 128:192], kt2[:, hb:hb + 64],
                         VT[cc][:, h * 64:(h + 1) * 64], True, True,
                         (("a1_kt2", kc % 2), ("vt", cc, h // 8)), pdk,
                         tile_position=(0, hb) if hb else None)
                    k.v_tt(SF[h][hb:hb + 64, :], SF[h][hb:hb + 64, :], pd[hb:hb + 64, 128:192],
                           ALU.add, reads=(("sf", h), pdk), writes=(("sf", h),))
                    k.v_copy(SB[h][hb:hb + 64, :], SF[h][hb:hb + 64, :],
                             reads=(("sf", h),), writes=(("sbp", h),))
        if s == 0:
            for t4 in range(4):
                k.dbg(f"ot{t4}", OT[t4][:, :], tuple(("ot", t4, h) for h in range(16)))
            k.dbg("sb0", SB[0][:, :], (("sbp", 0),))
            k.dbg("sb1", SB[1][:, :], (("sbp", 1),))
        if k.paranoid >= 2:
            k.tk.barrier()
        MF = VT
        for cc in range(CC_N):
            for ec in range(NCHUNK):
                ptp, ptpk = k.pt[(cc * NCHUNK + ec) % 2], k.ptk[(cc * NCHUNK + ec) % 2]
                k.tr(ptp[:, 0:128], OT[cc][:, ec * 128:(ec + 1) * 128],
                     tuple(("ot", cc, h) for h in range(ec * 2, ec * 2 + 2)), ptpk)
                k.v_copy(MF[ec // 2][:, (ec % 2) * 512 + cc * 128:(ec % 2) * 512 + (cc + 1) * 128],
                         ptp[:, 0:128], reads=(ptpk,),
                         writes=(("mf", ec), ("vt", ec // 2, ec % 2)))
        for ec in range(NCHUNK):
            wt, wk = k.wload(pj[ec], 1024, "pj")
            pa, pak = k.pp[ec % 2], k.ppk[ec % 2]
            for kc in range(NCHUNK):
                k.mm(pa[:, :], wt[:, kc * 128:(kc + 1) * 128],
                     MF[kc // 2][:, (kc % 2) * 512:(kc % 2) * 512 + 512],
                     kc == 0, kc == 7, (wk, ("mf", kc)), pak)
            k.v_tt(k.ST[ec][:, sl], k.ST[ec][:, sl], pa[:, :], ALU.add,
                   reads=(("st", ec, s), pak), writes=(("st", ec, s),))
        k.ln_e(s)


def emit_l0_attn(k: KB):
    cw = k.din("a0_convw", [8, 128, 4])
    qkg = k.din("a0_qkg", [4, 128, 2048])
    wv = k.din("a0_wv", [1024, 1024])
    wg = k.din("a0_wg", [1024, 1024])
    wo = k.din("a0_wout", [8, 128, 1024])
    bg = k.din("a0_bg", [1, 1024])
    aug = k.din("a0_aug", [128, 8])

    CW = k.sba("a0_cw", [128, 32], F32)
    for c in range(NCHUNK):
        k.dma("sp", CW[:, c * 4:(c + 1) * 4], cw[c], writes=(("cw", c),))
    BGR = k.sba("a0_bgr", [1, 1024], BF16)
    k.dma("pool", BGR[:, :], bg, writes=("a0_bgr",))
    AUG = k.sba("a0_augt", [128, 8], BF16)
    k.dma("pool", AUG[:, :], aug, writes=("a0_augt",))

    SF2 = [k.sba(f"a0_sf{kc}", [128, 128], F32) for kc in range(4)]
    SBp = [k.sba(f"a0_sb{h}", [128, 128], BF16) for h in range(8)]
    for kc in range(4):
        k.memset(SF2[kc][:, :], 0.0, writes=(("a0sf", kc),), eng="pool")
    for h in range(8):
        k.memset(SBp[h][:, :], 0.0, writes=(("a0sb", h),), eng="pool")

    KGF = k.sba("a0_kgf", [128, 512], F32)
    SC1 = k.sba("a0_sc1", [128, 512], F32)
    SC2 = k.sba("a0_sc2", [128, 512], F32)
    GK = [k.sba("a0_gk0", [128, 512], F32)]
    G = [k.sba("a0_g0", [128, 512], F32)]
    KF = [k.sba("a0_kf0", [128, 512], F32)]
    EB = [k.sba("a0_eb0", [128, 512], F32)]
    QT = [k.sba(f"a0_qt{c}", [128, 512], BF16) for c in range(4)]
    KT = [k.sba(f"a0_kt{c}", [128, 512], BF16) for c in range(4)]
    KH = [k.sba(f"a0_kh{c}", [128, 512], BF16) for c in range(4)]
    U = [k.sba(f"a0_u{c}", [128, 512], BF16) for c in range(4)]
    BT = k.sba("a0_bt", [128, 16], F32)
    BCOL = k.sba("a0_bcol", [128, 4], F32)
    EH = KGF
    VT = [k.sba(f"a0_vt{t}", [128, 1024], BF16) for t in range(CC_N)]
    SILG = [k.sba(f"a0_sg{t}", [128, 1024], BF16) for t in range(CC_N)]
    OT = [k.sba(f"a0_ot{t}", [128, 1024], F32) for t in range(CC_N)]
    KHT = [k.sba(f"a0_kht{i}", [128, 128], BF16) for i in range(4)]
    MS = [k.sba(f"a0_ms{i}", [128, 128], BF16) for i in range(8)]
    SGP = [k.sba(f"a0_sgp{i}", [128, 256], F32) for i in range(2)]
    AW = k.sba("a0_aw", [128, 8], F32)
    DCOL = [k.sba(f"a0_dcol{i}", [128, 128], F32) for i in range(2)]
    MCOL = [k.sba(f"a0_mcol{i}", [128, 4], F32) for i in range(2)]
    VCOL = [k.sba(f"a0_vcol{i}", [128, 2], F32) for i in range(2)]
    SQD = DCOL
    XTAIL = k.sba("a0_xt", [128, 32], F32)
    mask = k.tensors["c_mask"]
    Z = k.tensors["c_zero"]

    def conv_stripe(s):
        sl = slice(s * ST_T, (s + 1) * ST_T)
        for c in range(NCHUNK):
            acc = SC1
            k.v_ts(acc[:, :], k.ST[c][:, sl], CW[:, c * 4 + 3:c * 4 + 4], None, ALU.mult, None,
                   reads=(("st", c, s), ("cw", c)), writes=("a0_sc1",))
            for jj in (2, 1, 0):
                off = 3 - jj
                k.v_stt(acc[:, off:512], k.ST[c][:, s * ST_T: s * ST_T + 512 - off],
                        CW[:, c * 4 + jj:c * 4 + jj + 1], acc[:, off:512], ALU.mult, ALU.add,
                        reads=(("st", c, s), ("cw", c), "a0_sc1"), writes=("a0_sc1",))
                if s > 0:
                    k.v_stt(acc[:, 0:off], XTAIL[:, c * 4 + 4 - off:c * 4 + 4],
                            CW[:, c * 4 + jj:c * 4 + jj + 1], acc[:, 0:off], ALU.mult, ALU.add,
                            reads=(("a0xt", c), ("cw", c), "a0_sc1"), writes=("a0_sc1",))
            k.actv(k.XBF[c][:, sl], acc[:, :], AF.Silu, reads=("a0_sc1",), writes=(("xbf", c, s),))

    for s in range(NSTRIPE):
        sl = slice(s * ST_T, (s + 1) * ST_T)
        conv_stripe(s)
        if s < NSTRIPE - 1:
            for c in range(NCHUNK):
                k.v_copy(XTAIL[:, c * 4:c * 4 + 4],
                         k.ST[c][:, (s + 1) * ST_T - 4:(s + 1) * ST_T],
                         reads=(("st", c, s),), writes=(("a0xt", c),))
        for kc in range(4):
            wt, wk = k.wload(qkg[kc], 2048, "qkg")
            par = kc % 2
            pa, pak = k.pp[par], k.ppk[par]
            for c in range(NCHUNK):
                k.mm(pa[:, :], wt[:, 1024 + c * 128:1024 + (c + 1) * 128], k.XBF[c][:, sl],
                     c == 0, c == 7, (wk, ("xbf", c, s)), pak)
            k.v_copy(KGF[:, :], pa[:, :], reads=(pak,), writes=("a0_kgf",))
            k.v_ts(SC2[:, :], KGF[:, :], 0.0, None, ALU.min, None, reads=("a0_kgf",), writes=("a0_sc2",))
            k.actv(SC1[:, :], KGF[:, :], AF.Abs, reads=("a0_kgf",), writes=("a0_sc1",))
            k.actv(SC1[:, :], SC1[:, :], AF.Exp, reads=("a0_sc1",), writes=("a0_sc1",), scale=-1.0)
            k.actv(SC1[:, :], SC1[:, :], AF.Ln, reads=("a0_sc1", "c_one"), writes=("a0_sc1",),
                   bias=k.tensors["c_one"][:, 0:1])
            gk = GK[0]
            k.v_stt(gk[:, :], SC1[:, :], -1.0, SC2[:, :], ALU.mult, ALU.add,
                    reads=("a0_sc1", "a0_sc2"), writes=("a0gk",))
            g_ = G[0]
            for cc in range(CC_N):
                k.v_scan(g_[:, cc * 128:(cc + 1) * 128], gk[:, cc * 128:(cc + 1) * 128],
                         Z[:, :], 0.0, ALU.add, ALU.add,
                         reads=("a0gk", "c_zero"), writes=("a0g",))
            if s == 0 and kc == 0 and k.debug:
                k.dbg("agk", gk[:, :], ("a0gk",))
                k.dbg("ag", g_[:, :], ("a0g",))
            kf = KF[0]
            k.actv(SC1[:, :], gk[:, :], AF.Exp, reads=("a0gk",), writes=("a0_sc1",),
                   scale=1.0 / 16)
            k.v_ts(kf[:, :], SC1[:, :], -1.0, 1.0, ALU.mult, ALU.add,
                   reads=("a0_sc1",), writes=("a0kf",))
            eb = EB[0]
            k.actv(eb[:, :], g_[:, :], AF.Exp, reads=("a0g",), writes=("a0eb",),
                   scale=1.0 / 16)
            k.actv(SC2[:, :], g_[:, :], AF.Exp, reads=("a0g",), writes=("a0_sc2",),
                   scale=-1.0 / 16)
            k.v_tt(KT[kc][:, :], kf[:, :], SC2[:, :], ALU.mult,
                   reads=("a0kf", "a0_sc2"), writes=(("a0kt", kc),))
            k.actv(BT[:, kc * 4:(kc + 1) * 4], g_[:, 127::128], AF.Exp,
                   reads=("a0g",), writes=("a0_bt",), scale=1.0 / 16)
            k.v_ts(BCOL[:, 0:4], g_[:, 127::128], 1.0 / 16, None, ALU.mult, None,
                   reads=("a0g",), writes=("a0_bcol",))
            for cc in range(CC_N):
                k.actv(EH[:, 0:128], g_[:, cc * 128:(cc + 1) * 128], AF.Exp,
                       reads=("a0g", "a0_bcol"), writes=("a0_eh", "a0_kgf"),
                       scale=-1.0 / 16, bias=BCOL[:, cc:cc + 1])
                k.v_tt(KH[kc][:, cc * 128:(cc + 1) * 128], kf[:, cc * 128:(cc + 1) * 128],
                       EH[:, 0:128], ALU.mult, reads=("a0kf", "a0_eh"), writes=(("a0kh", kc),))
            pb, pbk = k.pp[2 + par], k.ppk[2 + par]
            for c in range(NCHUNK):
                k.mm(pb[:, :], wt[:, c * 128:(c + 1) * 128], k.XBF[c][:, sl],
                     c == 0, c == 7, (wk, ("xbf", c, s)), pbk)
            k.v_stt(QT[kc][:, :], pb[:, :], 0.125, eb[:, :], ALU.mult, ALU.mult,
                    reads=(pbk, "a0eb"), writes=(("a0qt", kc),))
            k.v_tt(U[kc][:, :], pb[:, :], kf[:, :], ALU.mult,
                   reads=(pbk, "a0kf"), writes=(("a0u", kc),))
        onesbf = k.tensors["c_onesbf"]
        for dst, wsrc, tag, with_bias in ((VT, wv, "a0vt", False), (SILG, wg, "a0sg", True)):
            tiles = []
            for g4 in range(2):
                wt, wk = k.wload(wsrc[g4 * 512:(g4 + 1) * 512, :]
                                 .rearrange("(a p) m -> a p m", p=128).transpose([1, 0, 2]),
                                 4096, "wvg")
                tiles.append((wt, wk))
            for half in range(2):
                hsl = slice(half * 512, half * 512 + 512)
                for kc in range(NCHUNK):
                    wt, wk = tiles[kc // 4]
                    for tc in range(CC_N):
                        pa, pak = k.pp[tc], k.ppk[tc]
                        k.mm(pa[:, :],
                             k.XBF[kc][:, s * ST_T + tc * 128: s * ST_T + (tc + 1) * 128],
                             wt[:, (kc % 4) * 1024 + half * 512:(kc % 4) * 1024 + half * 512 + 512],
                             kc == 0, (kc == 7 and not with_bias), (wk, ("xbf", kc, s)), pak)
                if with_bias:
                    for tc in range(CC_N):
                        pa, pak = k.pp[tc], k.ppk[tc]
                        k.mm(pa[:, :], onesbf[0:1, :], BGR[0:1, half * 512:half * 512 + 512],
                             False, True, ("c_onesbf", "a0_bgr"), pak)
                for tc in range(CC_N):
                    if with_bias:
                        k.actv(SC1[:, :], k.pp[tc][:, :], AF.Sigmoid,
                               reads=(k.ppk[tc],), writes=("a0_sc1",))
                        k.v_tt(SILG[tc][:, hsl], SC1[:, :], k.pp[tc][:, :], ALU.mult,
                               reads=("a0_sc1", k.ppk[tc]), writes=((tag, tc, half),))
                    else:
                        k.v_copy(dst[tc][:, hsl], k.pp[tc][:, :], reads=(k.ppk[tc],),
                                 writes=((tag, tc, half),))
        for cc in range(CC_N):
            for kc in range(4):
                ptp, ptpk = k.pt[kc % 2], k.ptk[kc % 2]
                k.tr(ptp[:, 0:128], KH[kc][:, cc * 128:(cc + 1) * 128],
                     (("a0kh", kc),), ptpk)
                k.v_copy(KHT[kc][:, :], ptp[:, 0:128], reads=(ptpk,), writes=(("a0kht", kc),))
                paw, pawk = k.pp[2], k.ppk[2]
                k.mm(paw[:, 256 + kc * 2:258 + kc * 2], U[kc][:, cc * 128:(cc + 1) * 128],
                     AUG[:, kc * 2:kc * 2 + 2], True, True, (("a0u", kc), "a0_augt"), pawk)
            k.v_copy(AW[:, :], k.pp[2][:, 256:264], reads=(k.ppk[2],), writes=("a0_aw",))
            for h in range(8):
                kc, half = h // 2, h % 2
                hb = 64 * half
                psc, psck = k.pp[4 + h % 2], k.ppk[4 + h % 2]
                k.mm(psc[:, 0:128], KT[kc][hb:hb + 64, cc * 128:(cc + 1) * 128],
                     QT[kc][hb:hb + 64, cc * 128:(cc + 1) * 128], True, True,
                     ((("a0kt", kc)), ("a0qt", kc)), psck)
                k.v_tt(MS[h][:, :], psc[:, 0:128], mask[:, :], ALU.mult,
                       reads=(psck, "c_mask"), writes=(("a0ms", h),))
            for h in range(8):
                kc, half = h // 2, h % 2
                prot = kc % 2
                if half == 0:
                    sg = SGP[prot]
                    for hh in (h, h + 1):
                        k.v_ts(sg[:, (hh % 2) * 128:(hh % 2) * 128 + 128],
                               VT[cc][:, hh * 128:(hh + 1) * 128],
                               AW[:, kc * 2 + hh % 2:kc * 2 + hh % 2 + 1], None, ALU.mult, None,
                               reads=((("a0vt", cc, hh // 4)), "a0_aw"), writes=(("a0_sgp", prot),))
                    k.actv(sg[:, :], sg[:, :], AF.Sigmoid, reads=(("a0_sgp", prot),),
                           writes=(("a0_sgp", prot),))
                po, pok = k.pp[h % 4], k.ppk[h % 4]
                k.mm(po[:, 0:128], MS[h][:, :], VT[cc][:, h * 128:(h + 1) * 128], True, False,
                     (("a0ms", h), ("a0vt", cc, h // 4)), pok)
                k.mm(po[:, 0:128], QT[kc][:, cc * 128:(cc + 1) * 128], SBp[h][:, :],
                     False, True, ((("a0qt", kc)), ("a0sb", h)), pok)
                k.v_tt(OT[cc][:, h * 128:(h + 1) * 128], po[:, 0:128],
                       SGP[prot][:, half * 128:half * 128 + 128], ALU.add,
                       reads=(pok, ("a0_sgp", prot)), writes=(("a0ot", cc, h),))
            for h in range(8):
                kc, half = h // 2, h % 2
                hb = 64 * half
                pd, pdk = k.pp[2 + h % 2], k.ppk[2 + h % 2]
                k.mm(pd[hb:hb + 64, 0:128], KHT[kc][:, hb:hb + 64],
                     VT[cc][:, h * 128:(h + 1) * 128], True, True,
                     (("a0kht", kc), ("a0vt", cc, h // 4)), pdk,
                     tile_position=(0, hb) if hb else None)
                k.v_stt(SF2[kc][hb:hb + 64, :], SF2[kc][hb:hb + 64, :],
                        BT[hb:hb + 64, kc * 4 + cc:kc * 4 + cc + 1], pd[hb:hb + 64, 0:128],
                        ALU.mult, ALU.add, reads=(("a0sf", kc), "a0_bt", pdk), writes=(("a0sf", kc),))
                k.v_copy(SBp[h][hb:hb + 64, :], SF2[kc][hb:hb + 64, :],
                         reads=(("a0sf", kc),), writes=(("a0sb", h),))
        for cc in range(CC_N):
            for h in range(8):
                rot = h % 2
                osl = OT[cc][:, h * 128:(h + 1) * 128]
                mcol, dcol, vcol, sqd = MCOL[rot], DCOL[rot], VCOL[rot], DCOL[rot]
                k.v_red(mcol[:, 0:1], osl, reads=(("a0ot", cc, h),), writes=(("a0_mcol", rot),))
                k.actv(sqd[:, :], osl, AF.Square, reads=(("a0ot", cc, h),),
                       writes=(("a0_sqd", rot), ("a0_dcol", rot)), accum_out=vcol[:, 0:1])
                k.v_ts(mcol[:, 1:2], mcol[:, 0:1], 1.0 / 128, None, ALU.mult, None,
                       reads=(("a0_mcol", rot),), writes=(("a0_mcol", rot),))
                k.v_ts(mcol[:, 2:3], mcol[:, 1:2], mcol[:, 1:2], None, ALU.mult, None,
                       reads=(("a0_mcol", rot),), writes=(("a0_mcol", rot),))
                k.v_stt(vcol[:, 1:2], vcol[:, 0:1], 1.0 / 128, mcol[:, 2:3],
                        ALU.mult, ALU.subtract,
                        reads=(("a0_mcol", rot), ("a0_sqd", rot)), writes=(("a0_vcol", rot),))
                k.tk.op("act", lambda e, o=vcol[:, 1:2], i=vcol[:, 1:2],
                        b=k.tensors["c_eps5"][:, 0:1]:
                        e.activation(o, i, AF.Ln, bias=b, scale=1.0),
                        reads=(("a0_vcol", rot), "c_eps5"), writes=(("a0_vcol", rot),))
                k.actv(vcol[:, 1:2], vcol[:, 1:2], AF.Exp, reads=(("a0_vcol", rot),),
                       writes=(("a0_vcol", rot),), scale=-0.5)
                k.v_ts(dcol[:, :], osl, mcol[:, 1:2], vcol[:, 1:2], ALU.subtract, ALU.mult,
                       reads=(("a0ot", cc, h), ("a0_mcol", rot), ("a0_vcol", rot)),
                       writes=(("a0_dcol", rot),))
                k.v_tt(SILG[cc][:, h * 128:(h + 1) * 128], dcol[:, :],
                       SILG[cc][:, h * 128:(h + 1) * 128], ALU.mult,
                       reads=(("a0_dcol", rot), ("a0sg", cc, h // 4)),
                       writes=(("a0m", cc, h),))
        if s == 0 and k.debug:
            for c4 in range(2):
                k.dbg(f"xc{c4}", k.XBF[c4][:, sl], (("xbf", c4, 0),))
            k.dbg("qt0", QT[0][:, :], (("a0qt", 0),))
            k.dbg("kt0", KT[0][:, :], (("a0kt", 0),))
            k.dbg("kh0", KH[0][:, :], (("a0kh", 0),))
            k.dbg("u0", U[0][:, :], (("a0u", 0),))
            k.dbg("bt", BT[:, :], ("a0_bt",))
            k.dbg("sg0", SILG[0][:, :], (("a0sg", 0, 0), ("a0sg", 0, 1)))
            for t4 in range(4):
                k.dbg(f"aot{t4}", OT[t4][:, :], tuple(("a0ot", t4, h) for h in range(8)))
            k.dbg("asf0", SF2[0][:, :], (("a0sf", 0),))
        MF = VT
        for cc in range(CC_N):
            for ec in range(NCHUNK):
                ptp, ptpk = k.pt[(cc * NCHUNK + ec) % 2], k.ptk[(cc * NCHUNK + ec) % 2]
                k.tr(ptp[:, 0:128], SILG[cc][:, ec * 128:(ec + 1) * 128],
                     tuple(("a0m", cc, h) for h in (ec,)), ptpk)
                k.v_copy(MF[ec // 2][:, (ec % 2) * 512 + cc * 128:(ec % 2) * 512 + (cc + 1) * 128],
                         ptp[:, 0:128], reads=(ptpk,),
                         writes=(("a0mf", ec), ("a0vt", ec // 2, ec % 2)))
        for ec in range(NCHUNK):
            wt, wk = k.wload(wo[ec], 1024, "wo")
            pa, pak = k.pp[ec % 2], k.ppk[ec % 2]
            for kc in range(NCHUNK):
                k.mm(pa[:, :], wt[:, kc * 128:(kc + 1) * 128],
                     MF[kc // 2][:, (kc % 2) * 512:(kc % 2) * 512 + 512],
                     kc == 0, kc == 7, (wk, ("a0mf", kc)), pak)
            k.v_tt(k.ST[ec][:, sl], k.ST[ec][:, sl], pa[:, :], ALU.add,
                   reads=(("st", ec, s), pak), writes=(("st", ec, s),))
        k.ln_e(s)
        if k.paranoid:
            k.tk.barrier()


_BUILD_CACHE = {}


def build_program(blocks):
    blocks = tuple(blocks)
    if blocks in _BUILD_CACHE:
        return _BUILD_CACHE[blocks]
    k = KB(blocks)
    e5 = k.sb("c_eps5", [128, 1], F32)
    k.memset(e5[:, :], 1e-5, writes=("c_eps5",))
    k.setup_common()
    k.init_arena()
    emitters = {0: emit_l0_attn, 1: emit_l0_ffn, 2: emit_l1_attn, 3: emit_l1_ffn}
    for i, b in enumerate(blocks):
        emitters[b](k)
        if i < len(blocks) - 1:
            k.arena_reset()
    k.store_out()
    pass
    k.tk.finalize()
    _BUILD_CACHE[blocks] = k
    return k


def _tile_oc(wT, n_oc, n_kc):
    return np.ascontiguousarray(
        wT.reshape(n_kc, 128, n_oc, 128).transpose(2, 1, 0, 3).reshape(n_oc, 128, n_kc * 128))


def prep_weights(inp, blocks):
    f32 = np.float32
    d = {}
    if 0 in blocks:
        d["a0_convw"] = np.ascontiguousarray(inp["L0_attn_conv_w"].reshape(8, 128, 4))
        wq = np.asarray(inp["L0_attn_wq"], f32)
        wkg = np.asarray(inp["L0_attn_wkg"], f32)
        qt = _tile_oc(wq, 4, 8)
        kgt = _tile_oc(wkg, 4, 8)
        d["a0_qkg"] = np.ascontiguousarray(np.concatenate([qt, kgt], axis=2))
        d["a0_wv"] = np.ascontiguousarray(np.asarray(inp["L0_attn_wv"], f32))
        d["a0_wg"] = np.ascontiguousarray(np.asarray(inp["L0_attn_wg"], f32))
        d["a0_wout"] = _tile_oc(np.asarray(inp["L0_attn_wout"], f32), 8, 8)
        d["a0_bg"] = np.ascontiguousarray(np.asarray(inp["L0_attn_bg"], f32).reshape(1, 1024))
        aug = np.zeros((128, 8), f32)
        a8 = np.asarray(inp["L0_attn_aug_balance"], f32).reshape(8, 64)
        for kc in range(4):
            aug[0:64, 2 * kc] = a8[2 * kc]
            aug[64:128, 2 * kc + 1] = a8[2 * kc + 1]
        d["a0_aug"] = aug
    if 1 in blocks:
        w1t = _tile_oc(np.asarray(inp["L0_ffn_w1_w"], f32).T, 32, 8)
        w2t = _tile_oc(np.asarray(inp["L0_ffn_w2_w"], f32).T, 32, 8)
        d["f0_w12"] = np.ascontiguousarray(np.concatenate([w1t, w2t], axis=2))
        d["f0_h1"] = _tile_oc(np.asarray(inp["L0_ffn_h1_w"], f32).T, 32, 32)
        d["f0_h2"] = _tile_oc(np.asarray(inp["L0_ffn_h2_w"], f32).T, 32, 32)
        d["f0_w3"] = _tile_oc(np.asarray(inp["L0_ffn_w3_w"], f32).T, 8, 32)
        s1 = np.asarray(inp["L0_ffn_w1_s"], f32).reshape(32, 128).T
        s2 = np.asarray(inp["L0_ffn_w2_s"], f32).reshape(32, 128).T
        d["f0_s12"] = np.ascontiguousarray(np.concatenate([s1, s2], axis=1))
        h1s = np.asarray(inp["L0_ffn_h1_s"], f32).reshape(32, 128).T
        h2s = np.asarray(inp["L0_ffn_h2_s"], f32).reshape(32, 128).T
        d["f0_sh"] = np.ascontiguousarray(np.concatenate([h1s, h2s], axis=1))
        d["f0_s3"] = np.ascontiguousarray(np.asarray(inp["L0_ffn_w3_s"], f32).reshape(8, 128).T)
    if 2 in blocks:
        qkv = np.asarray(inp["L1_attn_qkv_w"], f32)
        d["a1_q"] = _tile_oc(qkv[0:1024].T, 8, 8)
        d["a1_k"] = _tile_oc(qkv[1024:2048].T, 8, 8)
        d["a1_v"] = np.ascontiguousarray(qkv[2048:3072].T)
        d["a1_proj"] = _tile_oc(np.asarray(inp["L1_attn_proj_w"], f32).T, 8, 8)
        d["a1_ab"] = np.ascontiguousarray(np.stack(
            [np.asarray(inp["L1_attn_alpha_log"], f32),
             np.asarray(inp["L1_attn_beta_log"], f32)], axis=1))
    if 3 in blocks:
        rm = np.asarray(inp["L1_ffn_rail_maker"], f32).reshape(4, 8, 8, 16, 16)
        mkb = np.zeros((4, 8, 128, 128), f32)
        for rl in range(8):
            mkb[:, :, rl * 16:(rl + 1) * 16, rl * 16:(rl + 1) * 16] = rm[:, :, rl]
        d["f1_mk"] = np.ascontiguousarray(mkb.transpose(0, 2, 1, 3).reshape(4, 128, 1024))
        ru = np.asarray(inp["L1_ffn_rail_up_proj"], f32).reshape(4, 8, 8, 64, 16)
        upb = np.zeros((4, 8, 128, 512), f32)
        for rl in range(8):
            upb[:, :, rl * 16:(rl + 1) * 16, rl * 64:(rl + 1) * 64] = \
                np.swapaxes(ru[:, :, rl], -1, -2)
        d["f1_up"] = np.ascontiguousarray(upb.transpose(0, 2, 1, 3).reshape(4, 128, 4096))
        cpt = np.asarray(inp["L1_ffn_w_cproj"], f32).T
        d["f1_cproj"] = np.ascontiguousarray(
            cpt.reshape(64, 2, 128, 1024).transpose(0, 2, 1, 3).reshape(64, 128, 2048))
    d["c_idbf"] = np.eye(128, dtype=ml_dtypes.bfloat16)
    d["c_mask"] = np.triu(np.ones((128, 128), ml_dtypes.bfloat16))
    d["c_ones"] = np.full((128, 128), 1.0 / 1024, ml_dtypes.bfloat16)
    d["c_onesrow"] = np.ones((1, 128), f32)
    d["c_onesbf"] = np.ones((1, 128), ml_dtypes.bfloat16)
    d["c_pos"] = (np.arange(16, dtype=f32)[None, :] * 128
                  + np.arange(128, dtype=f32)[:, None] + 1.0)
    return d


def kernel(**inputs):
    x = np.asarray(inputs["L0_attn_x"], np.float32)
    xT = [np.ascontiguousarray(x[b].T) for b in range(B)]
    for grp in GROUPS:
        kb = build_program(grp)
        wd = prep_weights(inputs, grp)
        in_maps = []
        for b in range(B):
            m = {"x_in": xT[b]}
            for n in kb.in_names:
                if n != "x_in":
                    m[n] = wd[n]
            in_maps.append(m)
        res = run_bass_kernel_spmd(kb.nc, in_maps, core_ids=list(range(B)))
        xT = [np.asarray(res.results[b]["x_out"], np.float32) for b in range(B)]
    out = np.stack([xTb.T for xTb in xT], axis=0)
    return np.ascontiguousarray(out)


# version 09
# speedup vs baseline: 1.0569x; 1.0569x over previous
import numpy as np
import ml_dtypes
import concourse.bass as bass
import concourse.mybir as mybir
from concourse.bass_utils import run_bass_kernel_spmd

F32 = mybir.dt.float32
BF16 = mybir.dt.bfloat16
AF = mybir.ActivationFunctionType
ALU = mybir.AluOpType
AX = mybir.AxisListType

B, T, E = 8, 2048, 1024
NCHUNK = E // 128
ST_T = 512
NSTRIPE = T // ST_T
CC_N = ST_T // 128

GROUPS = [[0, 1, 2, 3]]

ENGS = ("sp", "pool", "te", "act", "dve")


class Tracker:
    N_LANES = {"pool": 8, "sp": 4, "te": 1, "act": 2, "dve": 1}

    def __init__(self, nc):
        self.nc = nc
        self.sem = {}
        self.cnt = {}
        for e in ENGS:
            self.sem[e] = nc.alloc_semaphore(name=f"tk_{e}")
            self.cnt[e] = 0
            for l in range(self.N_LANES[e]):
                n = f"{e}_d{l}"
                self.sem[n] = nc.alloc_semaphore(name=f"tk_{n}")
                self.cnt[n] = 0
        self.lane_i = {e: 0 for e in ENGS}
        self._own_waited = {}
        self.clock = {e: {s: 0 for s in self.sem} for e in ENGS}
        self.snap = {}
        self.w = {}
        self.r = {}
        self.prog = {e: [] for e in ENGS}
        self.nops = 0

    STRICT_SAME = ("dve", "act", "pool")

    def _collect(self, eng, reads, writes, self_dma, strict=False):
        need = {}
        strict = strict or eng in self.STRICT_SAME

        def add(tok):
            pe, ps, pv, pdma = tok
            if pe == eng and not pdma and not strict:
                return
            if need.get(ps, 0) < pv:
                need[ps] = pv

        for b in reads:
            lw = self.w.get(b)
            if lw is not None:
                add(lw)
        for b in writes:
            lw = self.w.get(b)
            if lw is not None:
                add(lw)
            for (pe, ps), (pv, pdma) in self.r.get(b, {}).items():
                if pe == eng and not pdma and not strict:
                    continue
                if need.get(ps, 0) < pv:
                    need[ps] = pv
        return need

    def _apply_waits(self, eng, need):
        waits = []
        ck = self.clock[eng]
        for s, v in sorted(need.items()):
            if s == eng:
                if v > self._own_waited.get(eng, 0):
                    waits.append((s, v))
                    self._own_waited[eng] = v
                continue
            if ck[s] >= v:
                continue
            waits.append((s, v))
            snap = self.snap.get((s, v))
            if snap:
                for s2, v2 in snap.items():
                    if ck[s2] < v2:
                        ck[s2] = v2
            if ck[s] < v:
                ck[s] = v
        return waits

    def op(self, eng, fn, reads=(), writes=(), dma=False, strict=False):
        need = self._collect(eng, reads, writes, dma, strict)
        attach = None
        if dma:
            lane = f"{eng}_d{self.lane_i[eng] % self.N_LANES[eng]}"
            self.lane_i[eng] += 1
            need.pop(lane, None)
            attach = (lane, self.cnt[lane])
            ck = self.clock[eng]
            if ck[lane] < self.cnt[lane]:
                ck[lane] = self.cnt[lane]
            waits = self._apply_waits(eng, need)
            self.cnt[lane] += 16
            v_after = self.cnt[lane]
            sem_used = lane
        else:
            waits = self._apply_waits(eng, need)
            self.cnt[eng] += 1
            v_after = self.cnt[eng]
            sem_used = eng
        if not dma:
            self.clock[eng][sem_used] = v_after
        snap = dict(self.clock[eng])
        snap[sem_used] = max(snap.get(sem_used, 0), v_after)
        self.snap[(sem_used, v_after)] = snap
        self.prog[eng].append((waits, fn, sem_used, 16 if dma else 1, attach))
        self.nops += 1
        tok = (eng, sem_used, v_after, dma)
        for b in writes:
            self.w[b] = tok
            self.r[b] = {}
        for b in reads:
            self.r.setdefault(b, {})[(eng, sem_used)] = (v_after, dma)
        return tok

    def barrier(self, engines=ENGS):
        totals = dict(self.cnt)
        for e in engines:
            need = {}
            for s, v in totals.items():
                if v > 0 and s != e:
                    need[s] = v
            waits = self._apply_waits(e, need)
            if waits:
                self.prog[e].append((waits, None, e, 0, None))

    def finalize(self):
        nc = self.nc
        secname = {"sp": "sync", "pool": "gpsimd", "te": "tensor", "act": "scalar", "dve": "vector"}
        with nc.Block() as blk:
            for e in ENGS:
                entries = self.prog[e]
                sems = self.sem

                def make(entries=entries):
                    def run(eng):
                        for waits, fn, sname, inc, attach in entries:
                            for (p, c) in waits:
                                eng.wait_ge(sems[p], c)
                            if fn is not None:
                                ins = fn(eng)
                                if attach is not None and attach[1] > 0:
                                    ins._wait_ge(sems[attach[0]], attach[1])
                                ins.then_inc(sems[sname], inc)
                    return run
                getattr(blk, secname[e])(make())


class KB:

    def __init__(self, blocks):
        nc = bass.Bass("TRN2", target_bir_lowering=False)
        self.nc = nc
        self.blocks = list(blocks)
        self.tk = Tracker(nc)
        self.sbuf_used = 0
        self.tensors = {}
        self.dram = {}
        self.in_names = []
        self.pp = [nc.alloc_psum_tensor(f"pp{i}", [128, 512], F32) for i in range(6)]
        self.pt = [nc.alloc_psum_tensor(f"pt{i}", [128, 1024], BF16) for i in range(2)]
        self.ppk = [f"pp{i}" for i in range(6)]
        self.ptk = [f"pt{i}" for i in range(2)]
        self._act_set = None
        self._uid = 0

    def sb(self, name, shape, dt):
        t = self.nc.alloc_sbuf_tensor("sb_" + name, list(shape), dt)
        per = int(np.prod(shape[1:])) * mybir.dt.size(dt)
        self.sbuf_used += (per + 31) // 32 * 32
        self.tensors[name] = t
        return t

    def init_arena(self):
        self.sb("lnsq", [128, 512], BF16)
        self.sb("lnrstd", [128, 512], F32)
        self.ARENA_B = self.nc.sbuf_bytes_remaining - 64
        self.arena_backing = self.nc.alloc_sbuf_tensor(
            "sb_arena", [128, self.ARENA_B], mybir.dt.uint8)
        self.sbuf_used += self.ARENA_B
        self.arena_base = self.nc.lookup_mloc(self.arena_backing).addr
        self.arena_off = 0
        self.phase = 0

    def sba(self, name, shape, dt):
        per = int(np.prod(shape[1:])) * mybir.dt.size(dt)
        per = (per + 31) // 32 * 32
        assert self.arena_off + per <= self.ARENA_B,             f"arena overflow at {name}: {self.arena_off + per}"
        t = self.nc.alloc_sbuf_tensor_at(
            f"sb_p{self.phase}_{name}", list(shape), dt,
            offset=self.arena_base + self.arena_off)
        self.arena_off += per
        return t

    def arena_reset(self):
        self.tk.barrier()
        self.arena_off = 0
        self.phase += 1

    def din(self, name, shape, dt=F32):
        h = self.nc.dram_tensor(name, list(shape), dt, kind="ExternalInput")
        self.dram[name] = h
        self.in_names.append(name)
        return h.ap()

    def dout(self, name, shape, dt=F32):
        h = self.nc.dram_tensor(name, list(shape), dt, kind="ExternalOutput")
        self.dram[name] = h
        return h.ap()

    def uid(self):
        self._uid += 1
        return self._uid

    debug = False
    paranoid = 0

    def dbg(self, name, ap, reads):
        if not self.debug:
            return
        shape = list(ap.shape)
        o = self.dout("dbg_" + name, shape, ap.dtype)
        self.dma("sp", o, ap, reads=tuple(reads), writes=(("dbg", name),))

    def dma(self, q, out_ap, in_ap, reads=(), writes=()):
        self.tk.op(q, lambda e, o=out_ap, i=in_ap: e.dma_start(out=o, in_=i),
                   reads=reads, writes=writes, dma=True)

    def mm(self, ps_ap, lhsT, rhs, start, stop, reads, pskey, tile_position=None):
        self.tk.op("te",
                   lambda e, o=ps_ap, l=lhsT, r=rhs, s=start, t=stop, tp=tile_position:
                   e.matmul(o, l, r, start=s, stop=t, tile_position=tp),
                   reads=tuple(reads), writes=(pskey,))

    def tr(self, ps_ap, in_ap, reads, pskey):
        idb = self.tensors["c_idbf"]
        self.tk.op("te", lambda e, o=ps_ap, i=in_ap, d=idb: e.transpose(o, i, d[:, :]),
                   reads=tuple(reads) + ("c_idbf",), writes=(pskey,))

    def actv(self, out_ap, in_ap, func, reads, writes, bias=0.0, scale=1.0, accum_out=None):
        self.tk.op("act",
                   lambda e, o=out_ap, i=in_ap, f=func, b=bias, s=scale, a=accum_out:
                   e.activation(o, i, f, bias=b, scale=s, accum_out=a)
                   if a is not None else e.activation(o, i, f, bias=b, scale=s),
                   reads=tuple(reads), writes=tuple(writes))

    def v_tt(self, out_ap, in0, in1, op, reads, writes, eng="dve"):
        self.tk.op(eng, lambda e, o=out_ap, a=in0, b=in1, p=op: e.tensor_tensor(o, a, b, p),
                   reads=tuple(reads), writes=tuple(writes))

    def v_ts(self, out_ap, in0, s1, s2, op0, op1, reads, writes, eng="dve",
             accum_out=None, sq=False):
        if sq:
            self.tk.op(eng, lambda e, o=out_ap, i=in0: e.tensor_mul(o, i, i),
                       reads=tuple(reads), writes=tuple(writes))
            return
        if op1 is None:
            self.tk.op(eng, lambda e, o=out_ap, i=in0, a=s1: e.tensor_scalar(o, i, a, None, op0),
                       reads=tuple(reads), writes=tuple(writes))
        else:
            self.tk.op(eng,
                       lambda e, o=out_ap, i=in0, a=s1, b=s2: e.tensor_scalar(o, i, a, b, op0, op1),
                       reads=tuple(reads), writes=tuple(writes))

    def v_stt(self, out_ap, in0, scalar, in1, op0, op1, reads, writes, eng="dve"):
        self.tk.op(eng,
                   lambda e, o=out_ap, a=in0, s=scalar, b=in1: e.scalar_tensor_tensor(o, a, s, b, op0, op1),
                   reads=tuple(reads), writes=tuple(writes))

    def v_copy(self, out_ap, in_ap, reads, writes, eng="dve"):
        if eng == "act":
            self.tk.op(eng, lambda e, o=out_ap, i=in_ap: e.activation(o, i, AF.Copy),
                       reads=tuple(reads), writes=tuple(writes))
        else:
            self.tk.op(eng, lambda e, o=out_ap, i=in_ap: e.tensor_copy(o, i),
                       reads=tuple(reads), writes=tuple(writes))

    def v_scan(self, out_ap, d0, d1, init, op0, op1, reads, writes):
        self.tk.op("dve",
                   lambda e, o=out_ap, a=d0, b=d1, i=init: e.tensor_tensor_scan(o, a, b, i, op0, op1),
                   reads=tuple(reads), writes=tuple(writes))

    def v_red(self, out_ap, in_ap, reads, writes):
        self.tk.op("dve", lambda e, o=out_ap, i=in_ap: e.reduce_sum(o, i, AX.X),
                   reads=tuple(reads), writes=tuple(writes))

    def memset(self, ap, val, writes, eng="dve"):
        self.tk.op(eng, lambda e, o=ap, v=val: e.memset(o, v), reads=(), writes=tuple(writes))

    def setup_common(self):
        nc = self.nc
        for n, sh, dt in [("c_idbf", [128, 128], BF16), ("c_mask", [128, 128], BF16),
                          ("c_ones", [128, 128], BF16), ("c_onesrow", [1, 128], F32),
                          ("c_onesbf", [1, 128], BF16), ("c_pos", [128, 16], F32)]:
            self.din(n, sh, dt)
            t = self.sb(n, sh, dt)
            self.dma("sp", t[:, :], self.dram[n].ap(), writes=(n,))
        z = self.sb("c_zero", [128, 128], F32)
        self.memset(z[:, :], 0.0, writes=("c_zero",))
        e1 = self.sb("c_one", [128, 1], F32)
        self.memset(e1[:, :], 1.0, writes=("c_one",))
        self.x_in = self.din("x_in", [E, T], F32)
        self.x_out = self.dout("x_out", [E, T], F32)
        self.ST = [self.sb(f"st{c}", [128, T], F32) for c in range(NCHUNK)]
        self.XBF = [self.sb(f"xbf{c}", [128, T], BF16) for c in range(NCHUNK)]
        xr = self.x_in.rearrange("(c p) t -> c p t", p=128)
        for c in range(NCHUNK):
            self.dma("sp", self.ST[c][:, :], xr[c],
                     writes=tuple(("st", c, s) for s in range(NSTRIPE)))
        self.ring = [self.sb(f"ring{i}", [128, 4096], BF16) for i in range(4)]
        self.ring_i = 0
        if self.blocks[0] != 0:
            for c in range(NCHUNK):
                self.v_copy(self.XBF[c][:, :], self.ST[c][:, :],
                            reads=tuple(("st", c, s) for s in range(NSTRIPE)),
                            writes=tuple(("xbf", c, s) for s in range(NSTRIPE)))

    def store_out(self):
        for c in range(NCHUNK):
            xo = self.x_out.rearrange("(c p) t -> c p t", p=128)
            self.dma("sp", xo[c], self.ST[c][:, :],
                     reads=tuple(("st", c, s) for s in range(NSTRIPE)),
                     writes=(("out", c),))
        waits = [(f"sp_d{l}", self.tk.cnt[f"sp_d{l}"])
                 for l in range(self.tk.N_LANES["sp"]) if self.tk.cnt[f"sp_d{l}"] > 0]
        self.tk.prog["sp"].append((waits, None, "sp", 0, None))

    def wload(self, src_ap, cols, tag):
        i = self.ring_i % len(self.ring)
        self.ring_i += 1
        key = ("ring", i)
        tile = self.ring[i]
        self.dma("pool", tile[:, 0:cols], src_ap, reads=(), writes=(key,))
        return tile, key

    def ln_e(self, s, y_psum_evacs_done_keys=()):
        sl = slice(s * ST_T, (s + 1) * ST_T)
        ones = self.tensors["c_ones"]
        st_keys = tuple(("st", c, s) for c in range(NCHUNK))
        for c in range(NCHUNK):
            self.v_copy(self.XBF[c][:, sl], self.ST[c][:, sl],
                        reads=(("st", c, s),), writes=(("xbf", c, s),), eng="act")
        sq = self.tensors["lnsq"]
        rstd = self.tensors["lnrstd"]
        for ss in range(ST_T // 512):
            s0 = s * ST_T + ss * 512
            ssl = slice(s0, s0 + 512)
            pm, pmk = self.pp[4], self.ppk[4]
            pv, pvk = self.pp[5], self.ppk[5]
            for c in range(NCHUNK):
                self.mm(pm[:, :], ones[:, :], self.XBF[c][:, ssl], c == 0, c == NCHUNK - 1,
                        ("c_ones", ("xbf", c, s)), pmk)
            for c in range(NCHUNK):
                self.v_tt(self.ST[c][:, ssl], self.ST[c][:, ssl], pm[:, :], ALU.subtract,
                          reads=(("st", c, s), pmk), writes=(("st", c, s),))
            for c in range(NCHUNK):
                self.actv(sq[:, :], self.ST[c][:, ssl], AF.Square,
                          reads=(("st", c, s),), writes=("lnsq",))
                self.mm(pv[:, :], ones[:, :], sq[:, :], c == 0, c == NCHUNK - 1,
                        ("c_ones", "lnsq"), pvk)
            self.actv(rstd[:, :], pv[:, :], AF.Ln, reads=(pvk, "c_eps5"), writes=("lnrstd",),
                      bias=self.tensors["c_eps5"][:, 0:1], scale=1.0)
            self.actv(rstd[:, :], rstd[:, :], AF.Exp, reads=("lnrstd",), writes=("lnrstd",),
                      scale=-0.5)
            for c in range(NCHUNK):
                self.v_tt(self.ST[c][:, ssl], self.ST[c][:, ssl], rstd[:, :], ALU.mult,
                          reads=(("st", c, s), "lnrstd"), writes=(("st", c, s),))
                self.v_copy(self.XBF[c][:, ssl], self.ST[c][:, ssl],
                            reads=(("st", c, s),), writes=(("xbf", c, s),), eng="act")


def emit_l0_ffn(k: KB):
    nc = k.nc
    w12 = k.din("f0_w12", [32, 128, 2048])
    h1 = k.din("f0_h1", [32, 128, 4096])
    h2 = k.din("f0_h2", [32, 128, 4096])
    w3 = k.din("f0_w3", [8, 128, 4096])
    s12 = k.din("f0_s12", [128, 64])
    sh = k.din("f0_sh", [128, 64])
    s3 = k.din("f0_s3", [128, 8])
    s12t = k.sba("f0_s12t", [128, 64], F32)
    sht = k.sba("f0_sht", [128, 64], F32)
    s3t = k.sba("f0_s3t", [128, 8], F32)
    k.dma("sp", s12t[:, :], s12, writes=("f0_s12t",))
    k.dma("sp", sht[:, :], sh, writes=("f0_sht",))
    k.dma("sp", s3t[:, :], s3, writes=("f0_s3t",))

    HA = [k.sba(f"f0_ha{i}", [128, 512], BF16) for i in range(32)]
    HB = [k.sba(f"f0_hb{i}", [128, 512], BF16) for i in range(32)]

    for s in range(NSTRIPE):
        sl = slice(s * ST_T, (s + 1) * ST_T)
        xkeys = tuple(("xbf", c, s) for c in range(NCHUNK))
        for oc in range(32):
            wt, wk = k.wload(w12[oc], 2048, "w12")
            pa, pak = k.pp[oc % 2], k.ppk[oc % 2]
            pb, pbk = k.pp[2 + oc % 2], k.ppk[2 + oc % 2]
            for kc in range(NCHUNK):
                k.mm(pa[:, :], wt[:, kc * 128:(kc + 1) * 128], k.XBF[kc][:, sl],
                     kc == 0, kc == 7, (wk, ("xbf", kc, s)), pak)
            for kc in range(NCHUNK):
                k.mm(pb[:, :], wt[:, 1024 + kc * 128:1024 + (kc + 1) * 128], k.XBF[kc][:, sl],
                     kc == 0, kc == 7, (wk, ("xbf", kc, s)), pbk)
            k.actv(HA[oc][:, :], pa[:, :], AF.Silu, reads=(pak, "f0_s12t"),
                   writes=(("ha", oc),), scale=s12t[:, oc:oc + 1])
            k.actv(HB[oc][:, :], pb[:, :], AF.Copy, reads=(pbk, "f0_s12t"),
                   writes=(("hb", oc),), scale=s12t[:, 32 + oc:33 + oc])
            k.v_tt(HA[oc][:, :], HA[oc][:, :], HB[oc][:, :], ALU.mult,
                   reads=(("ha", oc), ("hb", oc)), writes=(("ha", oc),))
        for oc in range(32):
            wt, wk = k.wload(h1[oc], 4096, "h1")
            pa, pak = k.pp[oc % 3], k.ppk[oc % 3]
            for kc in range(32):
                k.mm(pa[:, :], wt[:, kc * 128:(kc + 1) * 128], HA[kc][:, :],
                     kc == 0, kc == 31, (wk, ("ha", kc)), pak)
            k.actv(HB[oc][:, :], pa[:, :], AF.Gelu, reads=(pak, "f0_sht"),
                   writes=(("hb", oc),), scale=sht[:, oc:oc + 1])
        for oc in range(32):
            wt, wk = k.wload(h2[oc], 4096, "h2")
            pa, pak = k.pp[oc % 3], k.ppk[oc % 3]
            for kc in range(32):
                k.mm(pa[:, :], wt[:, kc * 128:(kc + 1) * 128], HB[kc][:, :],
                     kc == 0, kc == 31, (wk, ("hb", kc)), pak)
            k.actv(HA[oc][:, :], pa[:, :], AF.Gelu, reads=(pak, "f0_sht"),
                   writes=(("ha", oc),), scale=sht[:, 32 + oc:33 + oc])
        for ec in range(NCHUNK):
            wt, wk = k.wload(w3[ec], 4096, "w3")
            pa, pak = k.pp[ec % 3], k.ppk[ec % 3]
            for kc in range(32):
                k.mm(pa[:, :], wt[:, kc * 128:(kc + 1) * 128], HA[kc][:, :],
                     kc == 0, kc == 31, (wk, ("ha", kc)), pak)
            k.v_stt(k.ST[ec][:, sl], pa[:, :], s3t[:, ec:ec + 1], k.ST[ec][:, sl],
                    ALU.mult, ALU.add,
                    reads=(pak, "f0_s3t", ("st", ec, s)), writes=(("st", ec, s),))
        k.ln_e(s)


def emit_l1_ffn(k: KB):
    mk = k.din("f1_mk", [4, 128, 1024])
    up = k.din("f1_up", [4, 128, 4096])
    cp = k.din("f1_cproj", [64, 128, 2048])

    MK = [k.sba(f"f1_mk{h}", [128, 1024], BF16) for h in range(4)]
    for h in range(4):
        k.dma("pool", MK[h][:, :], mk[h], writes=(("mk", h),))
    UPT = k.sba("f1_upt", [128, 2048], BF16)
    Y = [k.sba(f"f1_y{i}", [128, 512], BF16) for i in range(32)]
    HS = [k.sba(f"f1_hs{i}", [128, 512], BF16) for i in range(16)]

    for s in range(NSTRIPE):
        sl = slice(s * ST_T, (s + 1) * ST_T)
        for h in range(4):
            for c in range(NCHUNK):
                i = h * 8 + c
                pa, pak = k.pp[i % 2], k.ppk[i % 2]
                k.mm(pa[:, :], MK[h][:, c * 128:(c + 1) * 128], k.XBF[c][:, sl],
                     True, True, (("mk", h), ("xbf", c, s)), pak)
                k.v_copy(Y[i][:, :], pa[:, :], reads=(pak,), writes=(("y", i),))
        for grp in range(8):
            hh = grp // 2
            k.dma("pool", UPT[:, :], up[hh][:, (grp % 2) * 2048:(grp % 2) * 2048 + 2048],
                  writes=("f1_upt",))
            gtiles = []
            for l in range(4):
                p0 = grp * 8 + 2 * l
                wt, wk = k.wload(cp[p0:p0 + 2].transpose([1, 0, 2]), 4096, "cp")
                gtiles.append((wt, wk))
            for j in range(16):
                hc = grp * 16 + j
                rem = hc % 32
                cpr, msub = rem // 4, rem % 4
                i = hh * 8 + cpr
                upcol = (cpr % 4) * 512 + msub * 128
                pa, pak = k.pp[4 + j % 2], k.ppk[4 + j % 2]
                k.mm(pa[:, :], UPT[:, upcol: upcol + 128],
                     Y[i][:, :], True, True, ("f1_upt", ("y", i)), pak)
                k.actv(HS[j][:, :], pa[:, :], AF.Gelu, reads=(pak,), writes=(("hs", j),))
            for half in range(2):
                for j in range(16):
                    wt, wk = gtiles[j // 4]
                    jj = j % 4
                    base = (jj // 2) * 2048 + (jj % 2) * 1024
                    for ec in range(half * 4, half * 4 + 4):
                        pa, pak = k.pp[ec % 4], k.ppk[ec % 4]
                        k.mm(pa[:, :], wt[:, base + ec * 128: base + (ec + 1) * 128],
                             HS[j][:, :], j == 0, j == 15, (wk, ("hs", j)), pak)
                for ec in range(half * 4, half * 4 + 4):
                    pa, pak = k.pp[ec % 4], k.ppk[ec % 4]
                    k.v_tt(k.ST[ec][:, sl], k.ST[ec][:, sl], pa[:, :], ALU.add,
                           reads=(("st", ec, s), pak), writes=(("st", ec, s),))
        k.ln_e(s)


def emit_l1_attn(k: KB):
    qt = k.din("a1_q", [8, 128, 1024])
    kt = k.din("a1_k", [8, 128, 1024])
    vt = k.din("a1_v", [1024, 1024])
    pj = k.din("a1_proj", [8, 128, 1024])
    alb = k.din("a1_ab", [16, 2])
    ab = k.sba("a1_ab", [16, 2], F32)
    k.dma("sp", ab[:, :], alb, writes=("a1_ab",))
    scc = k.sba("a1_scc", [16, 2], F32)
    scr = k.sba("a1_scr", [1, 16], F32)
    tmp = k.sba("a1_tmp", [16, 2], F32)
    mx = k.sba("a1_mx", [16, 2], F32)
    one = k.tensors["c_one"]
    k.actv(tmp[:, :], ab[:, :], AF.Abs, reads=("a1_ab",), writes=("a1_tmp",))
    k.actv(tmp[:, :], tmp[:, :], AF.Exp, reads=("a1_tmp",), writes=("a1_tmp",), scale=-1.0)
    k.actv(tmp[:, :], tmp[:, :], AF.Ln, reads=("a1_tmp", "c_one"), writes=("a1_tmp",),
           bias=one[0:16, 0:1])
    k.v_ts(mx[:, :], ab[:, :], 0.0, None, ALU.max, None, reads=("a1_ab",), writes=("a1_mx",))
    for c2 in range(2):
        k.actv(scc[:, c2:c2 + 1], tmp[:, c2:c2 + 1], AF.Relu,
               reads=("a1_tmp", "a1_mx"), writes=("a1_scc",), bias=mx[:, c2:c2 + 1])
        k.v_ts(scc[:, c2:c2 + 1], scc[:, c2:c2 + 1], 1e-6, None, ALU.add, None,
               reads=("a1_scc",), writes=("a1_scc",))
    k.v_ts(scc[:, 0:1], scc[:, 0:1], scc[:, 1:2], 8.0, ALU.mult, ALU.mult,
           reads=("a1_scc",), writes=("a1_scc",))
    k.dbg("scc", scc[:, :], ("a1_scc",))
    scds = k.nc.dram_tensor("a1_scds", [1, 16], F32, kind="Internal")
    k.dma("pool", scds.ap(), scc[0:16, 0:1], reads=("a1_scc",), writes=("a1_scds",))
    scbc = k.sba("a1_scbc", [128, 16], F32)
    k.dma("pool", scbc[:, :], scds.ap().broadcast_to([128, 16]),
          reads=("a1_scds",), writes=("a1_scbc",))
    k.dbg("scbc", scbc[:, :], ("a1_scbc",))
    RCP = [k.sba(f"a1_rcp{h}", [128, 16], F32) for h in range(16)]
    pos = k.tensors["c_pos"]
    dena = k.sba("a1_dena", [128, 16], F32)
    denb = k.sba("a1_denb", [128, 16], F32)
    k.dbg("cpos", k.tensors["c_pos"][:, :], ("c_pos",))
    for h in range(16):
        k.v_ts(dena[:, :], pos[:, :], scbc[:, h:h + 1], None, ALU.mult, None,
               reads=("c_pos", "a1_scbc"), writes=("a1_dena",))
        k.v_ts(denb[:, :], dena[:, :], 1e-6, None, ALU.max, None,
               reads=("a1_dena",), writes=("a1_denb",))
        if h == 0:
            k.dbg("dena", dena[:, :], ("a1_dena",))
            k.dbg("denb", denb[:, :], ("a1_denb",))
        k.tk.op("dve", lambda e, o=RCP[h][:, :], i=denb[:, :]: e.reciprocal(o, i),
                reads=("a1_denb",), writes=(("rcp", h),), strict=True)

    SF = [k.sba(f"a1_sf{h}", [128, 64], F32) for h in range(16)]
    SB = [k.sba(f"a1_sb{h}", [128, 64], BF16) for h in range(16)]
    for h in range(16):
        k.memset(SF[h][:, :], 0.0, writes=(("sf", h),), eng="pool")
        k.memset(SB[h][:, :], 0.0, writes=(("sbp", h),), eng="pool")

    PHQ = [k.sba(f"a1_phq{c}", [128, 512], BF16) for c in range(NCHUNK)]
    PHK = [k.sba(f"a1_phk{c}", [128, 512], BF16) for c in range(NCHUNK)]
    VT = [k.sba(f"a1_vt{t}", [128, 1024], BF16) for t in range(CC_N)]
    OT = [k.sba(f"a1_ot{t}", [128, 1024], BF16) for t in range(CC_N)]
    KT2 = [k.sba(f"a1_kt2{i}", [128, 128], BF16) for i in range(2)]
    MS = [k.sba(f"a1_ms{i}", [128, 128], BF16) for i in range(2)]
    mask = k.tensors["c_mask"]

    for s in range(NSTRIPE):
        sl = slice(s * ST_T, (s + 1) * ST_T)
        xkeys = tuple(("xbf", c, s) for c in range(NCHUNK))
        for part, tiles, src in (("q", PHQ, qt), ("k", PHK, kt)):
            for oc in range(NCHUNK):
                wt, wk = k.wload(src[oc], 1024, "qk")
                pa, pak = k.pp[oc % 2], k.ppk[oc % 2]
                for kc in range(NCHUNK):
                    k.mm(pa[:, :], wt[:, kc * 128:(kc + 1) * 128], k.XBF[kc][:, sl],
                         kc == 0, kc == 7, (wk, ("xbf", kc, s)), pak)
                k.v_ts(tiles[oc][:, :], pa[:, :], 0.0, None, ALU.max, None,
                       reads=(pak,), writes=((part, oc),))
                k.v_ts(tiles[oc][:, :], tiles[oc][:, :], 1e-6, None, ALU.add, None,
                       reads=((part, oc),), writes=((part, oc),))
        vtiles = []
        for g4 in range(2):
            wt, wk = k.wload(vt[g4 * 512:(g4 + 1) * 512, :].rearrange("(a p) m -> a p m", p=128).transpose([1, 0, 2]),
                             4096, "v")
            vtiles.append((wt, wk))
        for half in range(2):
            hsl = slice(half * 512, half * 512 + 512)
            for kc in range(NCHUNK):
                wt, wk = vtiles[kc // 4]
                for tc in range(CC_N):
                    pa, pak = k.pp[tc], k.ppk[tc]
                    k.mm(pa[:, :], k.XBF[kc][:, s * ST_T + tc * 128: s * ST_T + (tc + 1) * 128],
                         wt[:, (kc % 4) * 1024 + half * 512:(kc % 4) * 1024 + half * 512 + 512],
                         kc == 0, kc == 7, (wk, ("xbf", kc, s)), pak)
            for tc in range(CC_N):
                k.v_copy(VT[tc][:, hsl], k.pp[tc][:, :], reads=(k.ppk[tc],), writes=(("vt", tc, half),))
        if s == 0:
            k.dbg("phq0", PHQ[0][:, :], (("q", 0),))
            k.dbg("phk0", PHK[0][:, :], (("k", 0),))
            k.dbg("vt0", VT[0][:, :], (("vt", 0, 0), ("vt", 0, 1)))
            k.dbg("rcp0", RCP[0][:, :], (("rcp", 0),))
        if k.paranoid >= 2:
            k.tk.barrier()
        for cc in range(CC_N):
            g = s * CC_N + cc
            csl = slice(s * ST_T + cc * 128, s * ST_T + (cc + 1) * 128)
            if k.paranoid >= 3:
                k.tk.barrier()
            for kc in range(NCHUNK):
                ptp, ptpk = k.pt[kc % 2], k.ptk[kc % 2]
                k.tr(ptp[:, 0:128], PHK[kc][:, cc * 128:(cc + 1) * 128],
                     (("k", kc),), ptpk)
                kt2 = KT2[kc % 2]
                k.v_copy(kt2[:, :], ptp[:, 0:128], reads=(ptpk,), writes=(("a1_kt2", kc % 2),))
                for half in range(2):
                    h = kc * 2 + half
                    hb = 64 * half
                    rot = h % 2
                    psc, psck = k.pp[4 + rot], k.ppk[4 + rot]
                    k.mm(psc[:, 0:128], PHK[kc][hb:hb + 64, cc * 128:(cc + 1) * 128],
                         PHQ[kc][hb:hb + 64, cc * 128:(cc + 1) * 128], True, True,
                         (("k", kc), ("q", kc)), psck)
                    msb = MS[rot]
                    k.v_tt(msb[:, :], psc[:, 0:128], mask[:, :], ALU.mult,
                           reads=(psck, "c_mask"), writes=(("ms", rot),))
                    po, pok = k.pp[0 + rot], k.ppk[0 + rot]
                    k.mm(po[:, 0:64], msb[:, :], VT[cc][:, h * 64:(h + 1) * 64], True, False,
                         (("ms", rot), ("vt", cc, h // 8)), pok)
                    k.mm(po[:, 0:64], PHQ[kc][:, cc * 128:(cc + 1) * 128],
                         SB[h][:, :], False, True, (("q", kc), ("sbp", h)), pok)
                    k.v_ts(OT[cc][:, h * 64:(h + 1) * 64], po[:, 0:64], RCP[h][:, g:g + 1],
                           None, ALU.mult, None, reads=(pok, ("rcp", h)), writes=(("ot", cc, h),))
                    pd, pdk = k.pp[2 + rot], k.ppk[2 + rot]
                    k.mm(pd[hb:hb + 64, 128:192], kt2[:, hb:hb + 64],
                         VT[cc][:, h * 64:(h + 1) * 64], True, True,
                         (("a1_kt2", kc % 2), ("vt", cc, h // 8)), pdk,
                         tile_position=(0, hb) if hb else None)
                    k.v_tt(SF[h][hb:hb + 64, :], SF[h][hb:hb + 64, :], pd[hb:hb + 64, 128:192],
                           ALU.add, reads=(("sf", h), pdk), writes=(("sf", h),))
                    k.v_copy(SB[h][hb:hb + 64, :], SF[h][hb:hb + 64, :],
                             reads=(("sf", h),), writes=(("sbp", h),))
        if s == 0:
            for t4 in range(4):
                k.dbg(f"ot{t4}", OT[t4][:, :], tuple(("ot", t4, h) for h in range(16)))
            k.dbg("sb0", SB[0][:, :], (("sbp", 0),))
            k.dbg("sb1", SB[1][:, :], (("sbp", 1),))
        if k.paranoid >= 2:
            k.tk.barrier()
        MF = VT
        for cc in range(CC_N):
            for ec in range(NCHUNK):
                ptp, ptpk = k.pt[(cc * NCHUNK + ec) % 2], k.ptk[(cc * NCHUNK + ec) % 2]
                k.tr(ptp[:, 0:128], OT[cc][:, ec * 128:(ec + 1) * 128],
                     tuple(("ot", cc, h) for h in range(ec * 2, ec * 2 + 2)), ptpk)
                k.v_copy(MF[ec // 2][:, (ec % 2) * 512 + cc * 128:(ec % 2) * 512 + (cc + 1) * 128],
                         ptp[:, 0:128], reads=(ptpk,),
                         writes=(("mf", ec), ("vt", ec // 2, ec % 2)))
        for ec in range(NCHUNK):
            wt, wk = k.wload(pj[ec], 1024, "pj")
            pa, pak = k.pp[ec % 2], k.ppk[ec % 2]
            for kc in range(NCHUNK):
                k.mm(pa[:, :], wt[:, kc * 128:(kc + 1) * 128],
                     MF[kc // 2][:, (kc % 2) * 512:(kc % 2) * 512 + 512],
                     kc == 0, kc == 7, (wk, ("mf", kc)), pak)
            k.v_tt(k.ST[ec][:, sl], k.ST[ec][:, sl], pa[:, :], ALU.add,
                   reads=(("st", ec, s), pak), writes=(("st", ec, s),))
        k.ln_e(s)


def emit_l0_attn(k: KB):
    cw = k.din("a0_convw", [8, 128, 4])
    qkg = k.din("a0_qkg", [4, 128, 2048])
    wv = k.din("a0_wv", [1024, 1024])
    wg = k.din("a0_wg", [1024, 1024])
    wo = k.din("a0_wout", [8, 128, 1024])
    bg = k.din("a0_bg", [1, 1024])
    aug = k.din("a0_aug", [128, 8])

    CW = k.sba("a0_cw", [128, 32], F32)
    for c in range(NCHUNK):
        k.dma("sp", CW[:, c * 4:(c + 1) * 4], cw[c], writes=(("cw", c),))
    BGR = k.sba("a0_bgr", [1, 1024], BF16)
    k.dma("pool", BGR[:, :], bg, writes=("a0_bgr",))
    AUG = k.sba("a0_augt", [128, 8], BF16)
    k.dma("pool", AUG[:, :], aug, writes=("a0_augt",))

    SF2 = [k.sba(f"a0_sf{kc}", [128, 128], F32) for kc in range(4)]
    SBp = [k.sba(f"a0_sb{h}", [128, 128], BF16) for h in range(8)]
    for kc in range(4):
        k.memset(SF2[kc][:, :], 0.0, writes=(("a0sf", kc),), eng="pool")
    for h in range(8):
        k.memset(SBp[h][:, :], 0.0, writes=(("a0sb", h),), eng="pool")

    KGF = k.sba("a0_kgf", [128, 512], F32)
    SC1 = k.sba("a0_sc1", [128, 512], F32)
    GK = [k.sba("a0_gk0", [128, 512], F32)]
    G = [k.sba("a0_g0", [128, 512], F32)]
    KF = [k.sba("a0_kf0", [128, 512], F32)]
    EB = [k.sba("a0_eb0", [128, 512], F32)]
    QT = [k.sba(f"a0_qt{c}", [128, 512], BF16) for c in range(4)]
    KT = [k.sba(f"a0_kt{c}", [128, 512], BF16) for c in range(4)]
    KH = [k.sba(f"a0_kh{c}", [128, 512], BF16) for c in range(4)]
    U = [k.sba(f"a0_u{c}", [128, 512], BF16) for c in range(4)]
    BT = k.sba("a0_bt", [128, 16], F32)
    BCOL = k.sba("a0_bcol", [128, 4], F32)
    EH = KGF
    VT = [k.sba(f"a0_vt{t}", [128, 1024], BF16) for t in range(CC_N)]
    SILG = [k.sba(f"a0_sg{t}", [128, 1024], BF16) for t in range(CC_N)]
    OT = [k.sba(f"a0_ot{t}", [128, 1024], F32) for t in range(CC_N)]
    KHT = [k.sba(f"a0_kht{i}", [128, 128], BF16) for i in range(4)]
    MS = [k.sba(f"a0_ms{i}", [128, 128], BF16) for i in range(8)]
    SGP = [k.sba(f"a0_sgp{i}", [128, 256], F32) for i in range(2)]
    AW = k.sba("a0_aw", [128, 8], F32)
    DCOL = [k.sba(f"a0_dcol{i}", [128, 128], F32) for i in range(2)]
    MCOL = [k.sba(f"a0_mcol{i}", [128, 24], F32) for i in range(2)]
    VCOL = [k.sba(f"a0_vcol{i}", [128, 16], F32) for i in range(2)]
    SQD = DCOL
    XTAIL = k.sba("a0_xt", [128, 32], F32)
    mask = k.tensors["c_mask"]
    Z = k.tensors["c_zero"]

    def conv_stripe(s):
        sl = slice(s * ST_T, (s + 1) * ST_T)
        for c in range(NCHUNK):
            acc = SC1
            k.v_ts(acc[:, :], k.ST[c][:, sl], CW[:, c * 4 + 3:c * 4 + 4], None, ALU.mult, None,
                   reads=(("st", c, s), ("cw", c)), writes=("a0_sc1",))
            for jj in (2, 1, 0):
                off = 3 - jj
                k.v_stt(acc[:, off:512], k.ST[c][:, s * ST_T: s * ST_T + 512 - off],
                        CW[:, c * 4 + jj:c * 4 + jj + 1], acc[:, off:512], ALU.mult, ALU.add,
                        reads=(("st", c, s), ("cw", c), "a0_sc1"), writes=("a0_sc1",))
                if s > 0:
                    k.v_stt(acc[:, 0:off], XTAIL[:, c * 4 + 4 - off:c * 4 + 4],
                            CW[:, c * 4 + jj:c * 4 + jj + 1], acc[:, 0:off], ALU.mult, ALU.add,
                            reads=(("a0xt", c), ("cw", c), "a0_sc1"), writes=("a0_sc1",))
            k.actv(k.XBF[c][:, sl], acc[:, :], AF.Silu, reads=("a0_sc1",), writes=(("xbf", c, s),))

    for s in range(NSTRIPE):
        sl = slice(s * ST_T, (s + 1) * ST_T)
        conv_stripe(s)
        if s < NSTRIPE - 1:
            for c in range(NCHUNK):
                k.v_copy(XTAIL[:, c * 4:c * 4 + 4],
                         k.ST[c][:, (s + 1) * ST_T - 4:(s + 1) * ST_T],
                         reads=(("st", c, s),), writes=(("a0xt", c),))
        for kc in range(4):
            wt, wk = k.wload(qkg[kc], 2048, "qkg")
            par = kc % 2
            pa, pak = k.pp[par], k.ppk[par]
            for c in range(NCHUNK):
                k.mm(pa[:, :], wt[:, 1024 + c * 128:1024 + (c + 1) * 128], k.XBF[c][:, sl],
                     c == 0, c == 7, (wk, ("xbf", c, s)), pak)
            gk = GK[0]
            k.v_copy(KGF[:, :], pa[:, :], reads=(pak,), writes=("a0_kgf",))
            k.v_ts(gk[:, :], KGF[:, :], 0.0, None, ALU.min, None, reads=("a0_kgf",), writes=("a0gk",))
            k.actv(SC1[:, :], KGF[:, :], AF.Abs, reads=("a0_kgf",), writes=("a0_sc1",))
            k.actv(SC1[:, :], SC1[:, :], AF.Exp, reads=("a0_sc1",), writes=("a0_sc1",), scale=-1.0)
            k.actv(SC1[:, :], SC1[:, :], AF.Ln, reads=("a0_sc1", "c_one"), writes=("a0_sc1",),
                   bias=k.tensors["c_one"][:, 0:1])
            k.v_stt(gk[:, :], SC1[:, :], -1.0, gk[:, :], ALU.mult, ALU.add,
                    reads=("a0_sc1", "a0gk"), writes=("a0gk",))
            g_ = G[0]
            for cc in range(CC_N):
                k.v_scan(g_[:, cc * 128:(cc + 1) * 128], gk[:, cc * 128:(cc + 1) * 128],
                         Z[:, :], 0.0, ALU.add, ALU.add,
                         reads=("a0gk", "c_zero"), writes=("a0g",))
            if s == 0 and kc == 0 and k.debug:
                k.dbg("agk", gk[:, :], ("a0gk",))
                k.dbg("ag", g_[:, :], ("a0g",))
            kf = KF[0]
            k.actv(SC1[:, :], gk[:, :], AF.Exp, reads=("a0gk",), writes=("a0_sc1",),
                   scale=1.0 / 16)
            k.v_ts(kf[:, :], SC1[:, :], -1.0, 1.0, ALU.mult, ALU.add,
                   reads=("a0_sc1",), writes=("a0kf",))
            eb = EB[0]
            k.actv(eb[:, :], g_[:, :], AF.Exp, reads=("a0g",), writes=("a0eb",),
                   scale=1.0 / 16)
            k.actv(KGF[:, :], g_[:, :], AF.Exp, reads=("a0g",), writes=("a0_kgf",),
                   scale=-1.0 / 16)
            k.v_tt(KT[kc][:, :], kf[:, :], KGF[:, :], ALU.mult,
                   reads=("a0kf", "a0_kgf"), writes=(("a0kt", kc),))
            k.actv(BT[:, kc * 4:(kc + 1) * 4], g_[:, 127::128], AF.Exp,
                   reads=("a0g",), writes=("a0_bt",), scale=1.0 / 16)
            k.v_ts(BCOL[:, 0:4], g_[:, 127::128], 1.0 / 16, None, ALU.mult, None,
                   reads=("a0g",), writes=("a0_bcol",))
            for cc in range(CC_N):
                k.actv(EH[:, 0:128], g_[:, cc * 128:(cc + 1) * 128], AF.Exp,
                       reads=("a0g", "a0_bcol"), writes=("a0_eh", "a0_kgf"),
                       scale=-1.0 / 16, bias=BCOL[:, cc:cc + 1])
                k.v_tt(KH[kc][:, cc * 128:(cc + 1) * 128], kf[:, cc * 128:(cc + 1) * 128],
                       EH[:, 0:128], ALU.mult, reads=("a0kf", "a0_eh"), writes=(("a0kh", kc),))
            pb, pbk = k.pp[2 + par], k.ppk[2 + par]
            for c in range(NCHUNK):
                k.mm(pb[:, :], wt[:, c * 128:(c + 1) * 128], k.XBF[c][:, sl],
                     c == 0, c == 7, (wk, ("xbf", c, s)), pbk)
            k.v_stt(QT[kc][:, :], pb[:, :], 0.125, eb[:, :], ALU.mult, ALU.mult,
                    reads=(pbk, "a0eb"), writes=(("a0qt", kc),))
            k.v_tt(U[kc][:, :], pb[:, :], kf[:, :], ALU.mult,
                   reads=(pbk, "a0kf"), writes=(("a0u", kc),))
        onesbf = k.tensors["c_onesbf"]
        for dst, wsrc, tag, with_bias in ((VT, wv, "a0vt", False), (SILG, wg, "a0sg", True)):
            tiles = []
            for g4 in range(2):
                wt, wk = k.wload(wsrc[g4 * 512:(g4 + 1) * 512, :]
                                 .rearrange("(a p) m -> a p m", p=128).transpose([1, 0, 2]),
                                 4096, "wvg")
                tiles.append((wt, wk))
            for half in range(2):
                hsl = slice(half * 512, half * 512 + 512)
                for kc in range(NCHUNK):
                    wt, wk = tiles[kc // 4]
                    for tc in range(CC_N):
                        pa, pak = k.pp[tc], k.ppk[tc]
                        k.mm(pa[:, :],
                             k.XBF[kc][:, s * ST_T + tc * 128: s * ST_T + (tc + 1) * 128],
                             wt[:, (kc % 4) * 1024 + half * 512:(kc % 4) * 1024 + half * 512 + 512],
                             kc == 0, (kc == 7 and not with_bias), (wk, ("xbf", kc, s)), pak)
                if with_bias:
                    for tc in range(CC_N):
                        pa, pak = k.pp[tc], k.ppk[tc]
                        k.mm(pa[:, :], onesbf[0:1, :], BGR[0:1, half * 512:half * 512 + 512],
                             False, True, ("c_onesbf", "a0_bgr"), pak)
                for tc in range(CC_N):
                    if with_bias:
                        k.actv(SC1[:, :], k.pp[tc][:, :], AF.Sigmoid,
                               reads=(k.ppk[tc],), writes=("a0_sc1",))
                        k.v_tt(SILG[tc][:, hsl], SC1[:, :], k.pp[tc][:, :], ALU.mult,
                               reads=("a0_sc1", k.ppk[tc]), writes=((tag, tc, half),))
                    else:
                        k.v_copy(dst[tc][:, hsl], k.pp[tc][:, :], reads=(k.ppk[tc],),
                                 writes=((tag, tc, half),), eng="act")
        for cc in range(CC_N):
            for kc in range(4):
                ptp, ptpk = k.pt[kc % 2], k.ptk[kc % 2]
                k.tr(ptp[:, 0:128], KH[kc][:, cc * 128:(cc + 1) * 128],
                     (("a0kh", kc),), ptpk)
                k.v_copy(KHT[kc][:, :], ptp[:, 0:128], reads=(ptpk,), writes=(("a0kht", kc),))
                paw, pawk = k.pp[2], k.ppk[2]
                k.mm(paw[:, 256 + kc * 2:258 + kc * 2], U[kc][:, cc * 128:(cc + 1) * 128],
                     AUG[:, kc * 2:kc * 2 + 2], True, True, (("a0u", kc), "a0_augt"), pawk)
            k.v_copy(AW[:, :], k.pp[2][:, 256:264], reads=(k.ppk[2],), writes=("a0_aw",))
            for h in range(8):
                kc, half = h // 2, h % 2
                hb = 64 * half
                psc, psck = k.pp[4 + h % 2], k.ppk[4 + h % 2]
                k.mm(psc[:, 0:128], KT[kc][hb:hb + 64, cc * 128:(cc + 1) * 128],
                     QT[kc][hb:hb + 64, cc * 128:(cc + 1) * 128], True, True,
                     ((("a0kt", kc)), ("a0qt", kc)), psck)
                k.v_tt(MS[h][:, :], psc[:, 0:128], mask[:, :], ALU.mult,
                       reads=(psck, "c_mask"), writes=(("a0ms", h),))
            for h in range(8):
                kc, half = h // 2, h % 2
                prot = kc % 2
                if half == 0:
                    sg = SGP[prot]
                    for hh in (h, h + 1):
                        k.v_ts(sg[:, (hh % 2) * 128:(hh % 2) * 128 + 128],
                               VT[cc][:, hh * 128:(hh + 1) * 128],
                               AW[:, kc * 2 + hh % 2:kc * 2 + hh % 2 + 1], None, ALU.mult, None,
                               reads=((("a0vt", cc, hh // 4)), "a0_aw"), writes=(("a0_sgp", prot),))
                    k.actv(sg[:, :], sg[:, :], AF.Sigmoid, reads=(("a0_sgp", prot),),
                           writes=(("a0_sgp", prot),))
                po, pok = k.pp[h % 4], k.ppk[h % 4]
                k.mm(po[:, 0:128], MS[h][:, :], VT[cc][:, h * 128:(h + 1) * 128], True, False,
                     (("a0ms", h), ("a0vt", cc, h // 4)), pok)
                k.mm(po[:, 0:128], QT[kc][:, cc * 128:(cc + 1) * 128], SBp[h][:, :],
                     False, True, ((("a0qt", kc)), ("a0sb", h)), pok)
                k.v_tt(OT[cc][:, h * 128:(h + 1) * 128], po[:, 0:128],
                       SGP[prot][:, half * 128:half * 128 + 128], ALU.add,
                       reads=(pok, ("a0_sgp", prot)), writes=(("a0ot", cc, h),))
            for h in range(8):
                kc, half = h // 2, h % 2
                hb = 64 * half
                pd, pdk = k.pp[2 + h % 2], k.ppk[2 + h % 2]
                k.mm(pd[hb:hb + 64, 0:128], KHT[kc][:, hb:hb + 64],
                     VT[cc][:, h * 128:(h + 1) * 128], True, True,
                     (("a0kht", kc), ("a0vt", cc, h // 4)), pdk,
                     tile_position=(0, hb) if hb else None)
                k.v_stt(SF2[kc][hb:hb + 64, :], SF2[kc][hb:hb + 64, :],
                        BT[hb:hb + 64, kc * 4 + cc:kc * 4 + cc + 1], pd[hb:hb + 64, 0:128],
                        ALU.mult, ALU.add, reads=(("a0sf", kc), "a0_bt", pdk), writes=(("a0sf", kc),))
                k.v_copy(SBp[h][hb:hb + 64, :], SF2[kc][hb:hb + 64, :],
                         reads=(("a0sf", kc),), writes=(("a0sb", h),))
        for cc in range(CC_N):
            rot = cc % 2
            mcol, vcol = MCOL[rot], VCOL[rot]
            for h in range(8):
                osl = OT[cc][:, h * 128:(h + 1) * 128]
                k.v_red(mcol[:, h:h + 1], osl, reads=(("a0ot", cc, h),),
                        writes=(("a0_mcol", rot),))
                k.actv(DCOL[h % 2][:, :], osl, AF.Square, reads=(("a0ot", cc, h),),
                       writes=(("a0_sqd", rot), ("a0_dcol", h % 2)),
                       accum_out=vcol[:, h:h + 1])
            k.v_ts(mcol[:, 8:16], mcol[:, 0:8], 1.0 / 128, None, ALU.mult, None,
                   reads=(("a0_mcol", rot),), writes=(("a0_mcol", rot),))
            k.v_ts(mcol[:, 16:24], mcol[:, 8:16], None, None, ALU.mult, None,
                   reads=(("a0_mcol", rot),), writes=(("a0_mcol", rot),), sq=True)
            k.v_stt(vcol[:, 8:16], vcol[:, 0:8], 1.0 / 128, mcol[:, 16:24],
                    ALU.mult, ALU.subtract,
                    reads=(("a0_mcol", rot), ("a0_sqd", rot)), writes=(("a0_vcol", rot),))
            k.tk.op("act", lambda e, o=vcol[:, 8:16], i=vcol[:, 8:16],
                    b=k.tensors["c_eps5"][:, 0:1]:
                    e.activation(o, i, AF.Ln, bias=b, scale=1.0),
                    reads=(("a0_vcol", rot), "c_eps5"), writes=(("a0_vcol", rot),))
            k.actv(vcol[:, 8:16], vcol[:, 8:16], AF.Exp, reads=(("a0_vcol", rot),),
                   writes=(("a0_vcol", rot),), scale=-0.5)
            for h in range(8):
                osl = OT[cc][:, h * 128:(h + 1) * 128]
                dcol = DCOL[h % 2]
                k.v_ts(dcol[:, :], osl, mcol[:, 8 + h:9 + h], vcol[:, 8 + h:9 + h],
                       ALU.subtract, ALU.mult,
                       reads=(("a0ot", cc, h), ("a0_mcol", rot), ("a0_vcol", rot)),
                       writes=(("a0_dcol", h % 2),))
                k.v_tt(SILG[cc][:, h * 128:(h + 1) * 128], dcol[:, :],
                       SILG[cc][:, h * 128:(h + 1) * 128], ALU.mult,
                       reads=(("a0_dcol", h % 2), ("a0sg", cc, h // 4)),
                       writes=(("a0m", cc, h),))
        if s == 0 and k.debug:
            for c4 in range(2):
                k.dbg(f"xc{c4}", k.XBF[c4][:, sl], (("xbf", c4, 0),))
            k.dbg("qt0", QT[0][:, :], (("a0qt", 0),))
            k.dbg("kt0", KT[0][:, :], (("a0kt", 0),))
            k.dbg("kh0", KH[0][:, :], (("a0kh", 0),))
            k.dbg("u0", U[0][:, :], (("a0u", 0),))
            k.dbg("bt", BT[:, :], ("a0_bt",))
            k.dbg("sg0", SILG[0][:, :], (("a0sg", 0, 0), ("a0sg", 0, 1)))
            for t4 in range(4):
                k.dbg(f"aot{t4}", OT[t4][:, :], tuple(("a0ot", t4, h) for h in range(8)))
            k.dbg("asf0", SF2[0][:, :], (("a0sf", 0),))
        MF = VT
        for cc in range(CC_N):
            for ec in range(NCHUNK):
                ptp, ptpk = k.pt[(cc * NCHUNK + ec) % 2], k.ptk[(cc * NCHUNK + ec) % 2]
                k.tr(ptp[:, 0:128], SILG[cc][:, ec * 128:(ec + 1) * 128],
                     tuple(("a0m", cc, h) for h in (ec,)), ptpk)
                k.v_copy(MF[ec // 2][:, (ec % 2) * 512 + cc * 128:(ec % 2) * 512 + (cc + 1) * 128],
                         ptp[:, 0:128], reads=(ptpk,),
                         writes=(("a0mf", ec), ("a0vt", ec // 2, ec % 2)))
        for ec in range(NCHUNK):
            wt, wk = k.wload(wo[ec], 1024, "wo")
            pa, pak = k.pp[ec % 2], k.ppk[ec % 2]
            for kc in range(NCHUNK):
                k.mm(pa[:, :], wt[:, kc * 128:(kc + 1) * 128],
                     MF[kc // 2][:, (kc % 2) * 512:(kc % 2) * 512 + 512],
                     kc == 0, kc == 7, (wk, ("a0mf", kc)), pak)
            k.v_tt(k.ST[ec][:, sl], k.ST[ec][:, sl], pa[:, :], ALU.add,
                   reads=(("st", ec, s), pak), writes=(("st", ec, s),))
        k.ln_e(s)
        if k.paranoid:
            k.tk.barrier()


_BUILD_CACHE = {}


def build_program(blocks):
    blocks = tuple(blocks)
    if blocks in _BUILD_CACHE:
        return _BUILD_CACHE[blocks]
    k = KB(blocks)
    e5 = k.sb("c_eps5", [128, 1], F32)
    k.memset(e5[:, :], 1e-5, writes=("c_eps5",))
    k.setup_common()
    k.init_arena()
    emitters = {0: emit_l0_attn, 1: emit_l0_ffn, 2: emit_l1_attn, 3: emit_l1_ffn}
    for i, b in enumerate(blocks):
        emitters[b](k)
        if i < len(blocks) - 1:
            k.arena_reset()
    k.store_out()
    pass
    k.tk.finalize()
    _BUILD_CACHE[blocks] = k
    return k


def _tile_oc(wT, n_oc, n_kc):
    return np.ascontiguousarray(
        wT.reshape(n_kc, 128, n_oc, 128).transpose(2, 1, 0, 3).reshape(n_oc, 128, n_kc * 128))


def prep_weights(inp, blocks):
    f32 = np.float32
    d = {}
    if 0 in blocks:
        d["a0_convw"] = np.ascontiguousarray(inp["L0_attn_conv_w"].reshape(8, 128, 4))
        wq = np.asarray(inp["L0_attn_wq"], f32)
        wkg = np.asarray(inp["L0_attn_wkg"], f32)
        qt = _tile_oc(wq, 4, 8)
        kgt = _tile_oc(wkg, 4, 8)
        d["a0_qkg"] = np.ascontiguousarray(np.concatenate([qt, kgt], axis=2))
        d["a0_wv"] = np.ascontiguousarray(np.asarray(inp["L0_attn_wv"], f32))
        d["a0_wg"] = np.ascontiguousarray(np.asarray(inp["L0_attn_wg"], f32))
        d["a0_wout"] = _tile_oc(np.asarray(inp["L0_attn_wout"], f32), 8, 8)
        d["a0_bg"] = np.ascontiguousarray(np.asarray(inp["L0_attn_bg"], f32).reshape(1, 1024))
        aug = np.zeros((128, 8), f32)
        a8 = np.asarray(inp["L0_attn_aug_balance"], f32).reshape(8, 64)
        for kc in range(4):
            aug[0:64, 2 * kc] = a8[2 * kc]
            aug[64:128, 2 * kc + 1] = a8[2 * kc + 1]
        d["a0_aug"] = aug
    if 1 in blocks:
        w1t = _tile_oc(np.asarray(inp["L0_ffn_w1_w"], f32).T, 32, 8)
        w2t = _tile_oc(np.asarray(inp["L0_ffn_w2_w"], f32).T, 32, 8)
        d["f0_w12"] = np.ascontiguousarray(np.concatenate([w1t, w2t], axis=2))
        d["f0_h1"] = _tile_oc(np.asarray(inp["L0_ffn_h1_w"], f32).T, 32, 32)
        d["f0_h2"] = _tile_oc(np.asarray(inp["L0_ffn_h2_w"], f32).T, 32, 32)
        d["f0_w3"] = _tile_oc(np.asarray(inp["L0_ffn_w3_w"], f32).T, 8, 32)
        s1 = np.asarray(inp["L0_ffn_w1_s"], f32).reshape(32, 128).T
        s2 = np.asarray(inp["L0_ffn_w2_s"], f32).reshape(32, 128).T
        d["f0_s12"] = np.ascontiguousarray(np.concatenate([s1, s2], axis=1))
        h1s = np.asarray(inp["L0_ffn_h1_s"], f32).reshape(32, 128).T
        h2s = np.asarray(inp["L0_ffn_h2_s"], f32).reshape(32, 128).T
        d["f0_sh"] = np.ascontiguousarray(np.concatenate([h1s, h2s], axis=1))
        d["f0_s3"] = np.ascontiguousarray(np.asarray(inp["L0_ffn_w3_s"], f32).reshape(8, 128).T)
    if 2 in blocks:
        qkv = np.asarray(inp["L1_attn_qkv_w"], f32)
        d["a1_q"] = _tile_oc(qkv[0:1024].T, 8, 8)
        d["a1_k"] = _tile_oc(qkv[1024:2048].T, 8, 8)
        d["a1_v"] = np.ascontiguousarray(qkv[2048:3072].T)
        d["a1_proj"] = _tile_oc(np.asarray(inp["L1_attn_proj_w"], f32).T, 8, 8)
        d["a1_ab"] = np.ascontiguousarray(np.stack(
            [np.asarray(inp["L1_attn_alpha_log"], f32),
             np.asarray(inp["L1_attn_beta_log"], f32)], axis=1))
    if 3 in blocks:
        rm = np.asarray(inp["L1_ffn_rail_maker"], f32).reshape(4, 8, 8, 16, 16)
        mkb = np.zeros((4, 8, 128, 128), f32)
        for rl in range(8):
            mkb[:, :, rl * 16:(rl + 1) * 16, rl * 16:(rl + 1) * 16] = rm[:, :, rl]
        d["f1_mk"] = np.ascontiguousarray(mkb.transpose(0, 2, 1, 3).reshape(4, 128, 1024))
        ru = np.asarray(inp["L1_ffn_rail_up_proj"], f32).reshape(4, 8, 8, 64, 16)
        upb = np.zeros((4, 8, 128, 512), f32)
        for rl in range(8):
            upb[:, :, rl * 16:(rl + 1) * 16, rl * 64:(rl + 1) * 64] = \
                np.swapaxes(ru[:, :, rl], -1, -2)
        d["f1_up"] = np.ascontiguousarray(upb.transpose(0, 2, 1, 3).reshape(4, 128, 4096))
        cpt = np.asarray(inp["L1_ffn_w_cproj"], f32).T
        d["f1_cproj"] = np.ascontiguousarray(
            cpt.reshape(64, 2, 128, 1024).transpose(0, 2, 1, 3).reshape(64, 128, 2048))
    d["c_idbf"] = np.eye(128, dtype=ml_dtypes.bfloat16)
    d["c_mask"] = np.triu(np.ones((128, 128), ml_dtypes.bfloat16))
    d["c_ones"] = np.full((128, 128), 1.0 / 1024, ml_dtypes.bfloat16)
    d["c_onesrow"] = np.ones((1, 128), f32)
    d["c_onesbf"] = np.ones((1, 128), ml_dtypes.bfloat16)
    d["c_pos"] = (np.arange(16, dtype=f32)[None, :] * 128
                  + np.arange(128, dtype=f32)[:, None] + 1.0)
    return d


def kernel(**inputs):
    x = np.asarray(inputs["L0_attn_x"], np.float32)
    xT = [np.ascontiguousarray(x[b].T) for b in range(B)]
    for grp in GROUPS:
        kb = build_program(grp)
        wd = prep_weights(inputs, grp)
        in_maps = []
        for b in range(B):
            m = {"x_in": xT[b]}
            for n in kb.in_names:
                if n != "x_in":
                    m[n] = wd[n]
            in_maps.append(m)
        res = run_bass_kernel_spmd(kb.nc, in_maps, core_ids=list(range(B)))
        xT = [np.asarray(res.results[b]["x_out"], np.float32) for b in range(B)]
    out = np.stack([xTb.T for xTb in xT], axis=0)
    return np.ascontiguousarray(out)


# version 10 (final)
# speedup vs baseline: 1.0791x; 1.0210x over previous
import numpy as np
import ml_dtypes
import concourse.bass as bass
import concourse.mybir as mybir
from concourse.bass_utils import run_bass_kernel_spmd

F32 = mybir.dt.float32
BF16 = mybir.dt.bfloat16
AF = mybir.ActivationFunctionType
ALU = mybir.AluOpType
AX = mybir.AxisListType

B, T, E = 8, 2048, 1024
NCHUNK = E // 128
ST_T = 512
NSTRIPE = T // ST_T
CC_N = ST_T // 128

GROUPS = [[0, 1, 2, 3]]

ENGS = ("sp", "pool", "te", "act", "dve")


class Tracker:
    N_LANES = {"pool": 8, "sp": 4, "te": 1, "act": 2, "dve": 1}

    def __init__(self, nc):
        self.nc = nc
        self.sem = {}
        self.cnt = {}
        for e in ENGS:
            self.sem[e] = nc.alloc_semaphore(name=f"tk_{e}")
            self.cnt[e] = 0
            for l in range(self.N_LANES[e]):
                n = f"{e}_d{l}"
                self.sem[n] = nc.alloc_semaphore(name=f"tk_{n}")
                self.cnt[n] = 0
        self.lane_i = {e: 0 for e in ENGS}
        self._own_waited = {}
        self.clock = {e: {s: 0 for s in self.sem} for e in ENGS}
        self.snap = {}
        self.w = {}
        self.r = {}
        self.prog = {e: [] for e in ENGS}
        self.nops = 0

    STRICT_SAME = ("dve", "act", "pool")

    def _collect(self, eng, reads, writes, self_dma, strict=False):
        need = {}
        strict = strict or eng in self.STRICT_SAME

        def add(tok):
            pe, ps, pv, pdma = tok
            if pe == eng and not pdma and not strict:
                return
            if need.get(ps, 0) < pv:
                need[ps] = pv

        for b in reads:
            lw = self.w.get(b)
            if lw is not None:
                add(lw)
        for b in writes:
            lw = self.w.get(b)
            if lw is not None:
                add(lw)
            for (pe, ps), (pv, pdma) in self.r.get(b, {}).items():
                if pe == eng and not pdma and not strict:
                    continue
                if need.get(ps, 0) < pv:
                    need[ps] = pv
        return need

    def _apply_waits(self, eng, need):
        waits = []
        ck = self.clock[eng]
        for s, v in sorted(need.items()):
            if s == eng:
                if v > self._own_waited.get(eng, 0):
                    waits.append((s, v))
                    self._own_waited[eng] = v
                continue
            if ck[s] >= v:
                continue
            waits.append((s, v))
            snap = self.snap.get((s, v))
            if snap:
                for s2, v2 in snap.items():
                    if ck[s2] < v2:
                        ck[s2] = v2
            if ck[s] < v:
                ck[s] = v
        return waits

    def op(self, eng, fn, reads=(), writes=(), dma=False, strict=False):
        need = self._collect(eng, reads, writes, dma, strict)
        attach = None
        if dma:
            lane = f"{eng}_d{self.lane_i[eng] % self.N_LANES[eng]}"
            self.lane_i[eng] += 1
            need.pop(lane, None)
            attach = (lane, self.cnt[lane])
            ck = self.clock[eng]
            if ck[lane] < self.cnt[lane]:
                ck[lane] = self.cnt[lane]
            waits = self._apply_waits(eng, need)
            self.cnt[lane] += 16
            v_after = self.cnt[lane]
            sem_used = lane
        else:
            waits = self._apply_waits(eng, need)
            self.cnt[eng] += 1
            v_after = self.cnt[eng]
            sem_used = eng
        if not dma:
            self.clock[eng][sem_used] = v_after
        snap = dict(self.clock[eng])
        snap[sem_used] = max(snap.get(sem_used, 0), v_after)
        self.snap[(sem_used, v_after)] = snap
        self.prog[eng].append((waits, fn, sem_used, 16 if dma else 1, attach))
        self.nops += 1
        tok = (eng, sem_used, v_after, dma)
        for b in writes:
            self.w[b] = tok
            self.r[b] = {}
        for b in reads:
            self.r.setdefault(b, {})[(eng, sem_used)] = (v_after, dma)
        return tok

    def barrier(self, engines=ENGS):
        totals = dict(self.cnt)
        for e in engines:
            need = {}
            for s, v in totals.items():
                if v > 0 and s != e:
                    need[s] = v
            waits = self._apply_waits(e, need)
            if waits:
                self.prog[e].append((waits, None, e, 0, None))

    def finalize(self):
        nc = self.nc
        secname = {"sp": "sync", "pool": "gpsimd", "te": "tensor", "act": "scalar", "dve": "vector"}
        with nc.Block() as blk:
            for e in ENGS:
                entries = self.prog[e]
                sems = self.sem

                def make(entries=entries):
                    def run(eng):
                        for waits, fn, sname, inc, attach in entries:
                            for (p, c) in waits:
                                eng.wait_ge(sems[p], c)
                            if fn is not None:
                                ins = fn(eng)
                                if attach is not None and attach[1] > 0:
                                    ins._wait_ge(sems[attach[0]], attach[1])
                                ins.then_inc(sems[sname], inc)
                    return run
                getattr(blk, secname[e])(make())


class KB:

    def __init__(self, blocks):
        nc = bass.Bass("TRN2", target_bir_lowering=False)
        self.nc = nc
        self.blocks = list(blocks)
        self.tk = Tracker(nc)
        self.sbuf_used = 0
        self.tensors = {}
        self.dram = {}
        self.in_names = []
        self.pp = [nc.alloc_psum_tensor(f"pp{i}", [128, 512], F32) for i in range(6)]
        self.pt = [nc.alloc_psum_tensor(f"pt{i}", [128, 1024], BF16) for i in range(2)]
        self.ppk = [f"pp{i}" for i in range(6)]
        self.ptk = [f"pt{i}" for i in range(2)]
        self._act_set = None
        self._uid = 0

    def sb(self, name, shape, dt):
        t = self.nc.alloc_sbuf_tensor("sb_" + name, list(shape), dt)
        per = int(np.prod(shape[1:])) * mybir.dt.size(dt)
        self.sbuf_used += (per + 31) // 32 * 32
        self.tensors[name] = t
        return t

    def init_arena(self):
        self.sb("lnsq", [128, 512], BF16)
        self.sb("lnrstd", [128, 512], F32)
        self.ARENA_B = self.nc.sbuf_bytes_remaining - 64
        self.arena_backing = self.nc.alloc_sbuf_tensor(
            "sb_arena", [128, self.ARENA_B], mybir.dt.uint8)
        self.sbuf_used += self.ARENA_B
        self.arena_base = self.nc.lookup_mloc(self.arena_backing).addr
        self.arena_off = 0
        self.phase = 0

    def sba(self, name, shape, dt):
        per = int(np.prod(shape[1:])) * mybir.dt.size(dt)
        per = (per + 31) // 32 * 32
        assert self.arena_off + per <= self.ARENA_B,             f"arena overflow at {name}: {self.arena_off + per}"
        t = self.nc.alloc_sbuf_tensor_at(
            f"sb_p{self.phase}_{name}", list(shape), dt,
            offset=self.arena_base + self.arena_off)
        self.arena_off += per
        return t

    def arena_reset(self):
        self.tk.barrier()
        self.arena_off = 0
        self.phase += 1

    def din(self, name, shape, dt=F32):
        h = self.nc.dram_tensor(name, list(shape), dt, kind="ExternalInput")
        self.dram[name] = h
        self.in_names.append(name)
        return h.ap()

    def dout(self, name, shape, dt=F32):
        h = self.nc.dram_tensor(name, list(shape), dt, kind="ExternalOutput")
        self.dram[name] = h
        return h.ap()

    def uid(self):
        self._uid += 1
        return self._uid

    debug = False
    paranoid = 0

    def dbg(self, name, ap, reads):
        if not self.debug:
            return
        shape = list(ap.shape)
        o = self.dout("dbg_" + name, shape, ap.dtype)
        self.dma("sp", o, ap, reads=tuple(reads), writes=(("dbg", name),))

    def dma(self, q, out_ap, in_ap, reads=(), writes=()):
        self.tk.op(q, lambda e, o=out_ap, i=in_ap: e.dma_start(out=o, in_=i),
                   reads=reads, writes=writes, dma=True)

    def mm(self, ps_ap, lhsT, rhs, start, stop, reads, pskey, tile_position=None):
        self.tk.op("te",
                   lambda e, o=ps_ap, l=lhsT, r=rhs, s=start, t=stop, tp=tile_position:
                   e.matmul(o, l, r, start=s, stop=t, tile_position=tp),
                   reads=tuple(reads), writes=(pskey,))

    def tr(self, ps_ap, in_ap, reads, pskey):
        idb = self.tensors["c_idbf"]
        self.tk.op("te", lambda e, o=ps_ap, i=in_ap, d=idb: e.transpose(o, i, d[:, :]),
                   reads=tuple(reads) + ("c_idbf",), writes=(pskey,))

    def actv(self, out_ap, in_ap, func, reads, writes, bias=0.0, scale=1.0, accum_out=None):
        self.tk.op("act",
                   lambda e, o=out_ap, i=in_ap, f=func, b=bias, s=scale, a=accum_out:
                   e.activation(o, i, f, bias=b, scale=s, accum_out=a)
                   if a is not None else e.activation(o, i, f, bias=b, scale=s),
                   reads=tuple(reads), writes=tuple(writes))

    def v_tt(self, out_ap, in0, in1, op, reads, writes, eng="dve"):
        self.tk.op(eng, lambda e, o=out_ap, a=in0, b=in1, p=op: e.tensor_tensor(o, a, b, p),
                   reads=tuple(reads), writes=tuple(writes))

    def v_ts(self, out_ap, in0, s1, s2, op0, op1, reads, writes, eng="dve",
             accum_out=None, sq=False):
        if sq:
            self.tk.op(eng, lambda e, o=out_ap, i=in0: e.tensor_mul(o, i, i),
                       reads=tuple(reads), writes=tuple(writes))
            return
        if op1 is None:
            self.tk.op(eng, lambda e, o=out_ap, i=in0, a=s1: e.tensor_scalar(o, i, a, None, op0),
                       reads=tuple(reads), writes=tuple(writes))
        else:
            self.tk.op(eng,
                       lambda e, o=out_ap, i=in0, a=s1, b=s2: e.tensor_scalar(o, i, a, b, op0, op1),
                       reads=tuple(reads), writes=tuple(writes))

    def v_stt(self, out_ap, in0, scalar, in1, op0, op1, reads, writes, eng="dve"):
        self.tk.op(eng,
                   lambda e, o=out_ap, a=in0, s=scalar, b=in1: e.scalar_tensor_tensor(o, a, s, b, op0, op1),
                   reads=tuple(reads), writes=tuple(writes))

    def v_copy(self, out_ap, in_ap, reads, writes, eng="dve"):
        if eng == "act":
            self.tk.op(eng, lambda e, o=out_ap, i=in_ap: e.activation(o, i, AF.Copy),
                       reads=tuple(reads), writes=tuple(writes))
        else:
            self.tk.op(eng, lambda e, o=out_ap, i=in_ap: e.tensor_copy(o, i),
                       reads=tuple(reads), writes=tuple(writes))

    def v_scan(self, out_ap, d0, d1, init, op0, op1, reads, writes):
        self.tk.op("dve",
                   lambda e, o=out_ap, a=d0, b=d1, i=init: e.tensor_tensor_scan(o, a, b, i, op0, op1),
                   reads=tuple(reads), writes=tuple(writes))

    def v_red(self, out_ap, in_ap, reads, writes):
        self.tk.op("dve", lambda e, o=out_ap, i=in_ap: e.reduce_sum(o, i, AX.X),
                   reads=tuple(reads), writes=tuple(writes))

    def memset(self, ap, val, writes, eng="dve"):
        self.tk.op(eng, lambda e, o=ap, v=val: e.memset(o, v), reads=(), writes=tuple(writes))

    def setup_common(self):
        nc = self.nc
        for n, sh, dt in [("c_idbf", [128, 128], BF16), ("c_mask", [128, 128], BF16),
                          ("c_ones", [128, 128], BF16), ("c_onesrow", [1, 128], F32),
                          ("c_onesbf", [1, 128], BF16), ("c_pos", [128, 16], F32)]:
            self.din(n, sh, dt)
            t = self.sb(n, sh, dt)
            self.dma("sp", t[:, :], self.dram[n].ap(), writes=(n,))
        z = self.sb("c_zero", [128, 128], F32)
        self.memset(z[:, :], 0.0, writes=("c_zero",))
        e1 = self.sb("c_one", [128, 1], F32)
        self.memset(e1[:, :], 1.0, writes=("c_one",))
        self.x_in = self.din("x_in", [E, T], F32)
        self.x_out = self.dout("x_out", [E, T], F32)
        self.ST = [self.sb(f"st{c}", [128, T], F32) for c in range(NCHUNK)]
        self.XBF = [self.sb(f"xbf{c}", [128, T], BF16) for c in range(NCHUNK)]
        xr = self.x_in.rearrange("(c p) t -> c p t", p=128)
        for c in range(NCHUNK):
            self.dma("sp", self.ST[c][:, :], xr[c],
                     writes=tuple(("st", c, s) for s in range(NSTRIPE)))
        self.ring = [self.sb(f"ring{i}", [128, 4096], BF16) for i in range(4)]
        self.ring_i = 0
        if self.blocks[0] != 0:
            for c in range(NCHUNK):
                self.v_copy(self.XBF[c][:, :], self.ST[c][:, :],
                            reads=tuple(("st", c, s) for s in range(NSTRIPE)),
                            writes=tuple(("xbf", c, s) for s in range(NSTRIPE)))

    def store_out(self):
        for c in range(NCHUNK):
            xo = self.x_out.rearrange("(c p) t -> c p t", p=128)
            self.dma("sp", xo[c], self.ST[c][:, :],
                     reads=tuple(("st", c, s) for s in range(NSTRIPE)),
                     writes=(("out", c),))
        waits = [(f"sp_d{l}", self.tk.cnt[f"sp_d{l}"])
                 for l in range(self.tk.N_LANES["sp"]) if self.tk.cnt[f"sp_d{l}"] > 0]
        self.tk.prog["sp"].append((waits, None, "sp", 0, None))

    def wload(self, src_ap, cols, tag):
        i = self.ring_i % len(self.ring)
        self.ring_i += 1
        key = ("ring", i)
        tile = self.ring[i]
        self.dma("pool", tile[:, 0:cols], src_ap, reads=(), writes=(key,))
        return tile, key

    def ln_e(self, s, y_psum_evacs_done_keys=()):
        sl = slice(s * ST_T, (s + 1) * ST_T)
        ones = self.tensors["c_ones"]
        st_keys = tuple(("st", c, s) for c in range(NCHUNK))
        for c in range(NCHUNK):
            self.v_copy(self.XBF[c][:, sl], self.ST[c][:, sl],
                        reads=(("st", c, s),), writes=(("xbf", c, s),), eng="act")
        sq = self.tensors["lnsq"]
        rstd = self.tensors["lnrstd"]
        for ss in range(ST_T // 512):
            s0 = s * ST_T + ss * 512
            ssl = slice(s0, s0 + 512)
            pm, pmk = self.pp[4], self.ppk[4]
            pv, pvk = self.pp[5], self.ppk[5]
            for c in range(NCHUNK):
                self.mm(pm[:, :], ones[:, :], self.XBF[c][:, ssl], c == 0, c == NCHUNK - 1,
                        ("c_ones", ("xbf", c, s)), pmk)
            for c in range(NCHUNK):
                self.v_tt(self.ST[c][:, ssl], self.ST[c][:, ssl], pm[:, :], ALU.subtract,
                          reads=(("st", c, s), pmk), writes=(("st", c, s),))
            for c in range(NCHUNK):
                self.actv(sq[:, :], self.ST[c][:, ssl], AF.Square,
                          reads=(("st", c, s),), writes=("lnsq",))
                self.mm(pv[:, :], ones[:, :], sq[:, :], c == 0, c == NCHUNK - 1,
                        ("c_ones", "lnsq"), pvk)
            self.actv(rstd[:, :], pv[:, :], AF.Ln, reads=(pvk, "c_eps5"), writes=("lnrstd",),
                      bias=self.tensors["c_eps5"][:, 0:1], scale=1.0)
            self.actv(rstd[:, :], rstd[:, :], AF.Exp, reads=("lnrstd",), writes=("lnrstd",),
                      scale=-0.5)
            for c in range(NCHUNK):
                self.v_tt(self.ST[c][:, ssl], self.ST[c][:, ssl], rstd[:, :], ALU.mult,
                          reads=(("st", c, s), "lnrstd"), writes=(("st", c, s),))
                self.v_copy(self.XBF[c][:, ssl], self.ST[c][:, ssl],
                            reads=(("st", c, s),), writes=(("xbf", c, s),), eng="act")


def emit_l0_ffn(k: KB):
    nc = k.nc
    w12 = k.din("f0_w12", [32, 128, 2048])
    h1 = k.din("f0_h1", [32, 128, 4096])
    h2 = k.din("f0_h2", [32, 128, 4096])
    w3 = k.din("f0_w3", [8, 128, 4096])
    s12 = k.din("f0_s12", [128, 64])
    sh = k.din("f0_sh", [128, 64])
    s3 = k.din("f0_s3", [128, 8])
    s12t = k.sba("f0_s12t", [128, 64], F32)
    sht = k.sba("f0_sht", [128, 64], F32)
    s3t = k.sba("f0_s3t", [128, 8], F32)
    k.dma("sp", s12t[:, :], s12, writes=("f0_s12t",))
    k.dma("sp", sht[:, :], sh, writes=("f0_sht",))
    k.dma("sp", s3t[:, :], s3, writes=("f0_s3t",))

    HA = [k.sba(f"f0_ha{i}", [128, 512], BF16) for i in range(32)]
    HB = [k.sba(f"f0_hb{i}", [128, 512], BF16) for i in range(32)]

    for s in range(NSTRIPE):
        sl = slice(s * ST_T, (s + 1) * ST_T)
        xkeys = tuple(("xbf", c, s) for c in range(NCHUNK))
        for oc in range(32):
            wt, wk = k.wload(w12[oc], 2048, "w12")
            pa, pak = k.pp[oc % 2], k.ppk[oc % 2]
            pb, pbk = k.pp[2 + oc % 2], k.ppk[2 + oc % 2]
            for kc in range(NCHUNK):
                k.mm(pa[:, :], wt[:, kc * 128:(kc + 1) * 128], k.XBF[kc][:, sl],
                     kc == 0, kc == 7, (wk, ("xbf", kc, s)), pak)
            for kc in range(NCHUNK):
                k.mm(pb[:, :], wt[:, 1024 + kc * 128:1024 + (kc + 1) * 128], k.XBF[kc][:, sl],
                     kc == 0, kc == 7, (wk, ("xbf", kc, s)), pbk)
            k.actv(HA[oc][:, :], pa[:, :], AF.Silu, reads=(pak, "f0_s12t"),
                   writes=(("ha", oc),), scale=s12t[:, oc:oc + 1])
            k.actv(HB[oc][:, :], pb[:, :], AF.Copy, reads=(pbk, "f0_s12t"),
                   writes=(("hb", oc),), scale=s12t[:, 32 + oc:33 + oc])
            k.v_tt(HA[oc][:, :], HA[oc][:, :], HB[oc][:, :], ALU.mult,
                   reads=(("ha", oc), ("hb", oc)), writes=(("ha", oc),))
        for oc in range(32):
            wt, wk = k.wload(h1[oc], 4096, "h1")
            pa, pak = k.pp[oc % 3], k.ppk[oc % 3]
            for kc in range(32):
                k.mm(pa[:, :], wt[:, kc * 128:(kc + 1) * 128], HA[kc][:, :],
                     kc == 0, kc == 31, (wk, ("ha", kc)), pak)
            k.actv(HB[oc][:, :], pa[:, :], AF.Gelu, reads=(pak, "f0_sht"),
                   writes=(("hb", oc),), scale=sht[:, oc:oc + 1])
        for oc in range(32):
            wt, wk = k.wload(h2[oc], 4096, "h2")
            pa, pak = k.pp[oc % 3], k.ppk[oc % 3]
            for kc in range(32):
                k.mm(pa[:, :], wt[:, kc * 128:(kc + 1) * 128], HB[kc][:, :],
                     kc == 0, kc == 31, (wk, ("hb", kc)), pak)
            k.actv(HA[oc][:, :], pa[:, :], AF.Gelu, reads=(pak, "f0_sht"),
                   writes=(("ha", oc),), scale=sht[:, 32 + oc:33 + oc])
        for ec in range(NCHUNK):
            wt, wk = k.wload(w3[ec], 4096, "w3")
            pa, pak = k.pp[ec % 3], k.ppk[ec % 3]
            for kc in range(32):
                k.mm(pa[:, :], wt[:, kc * 128:(kc + 1) * 128], HA[kc][:, :],
                     kc == 0, kc == 31, (wk, ("ha", kc)), pak)
            k.v_stt(k.ST[ec][:, sl], pa[:, :], s3t[:, ec:ec + 1], k.ST[ec][:, sl],
                    ALU.mult, ALU.add,
                    reads=(pak, "f0_s3t", ("st", ec, s)), writes=(("st", ec, s),))
        k.ln_e(s)


def emit_l1_ffn(k: KB):
    mk = k.din("f1_mk", [4, 128, 1024])
    up = k.din("f1_up", [4, 128, 4096])
    cp = k.din("f1_cproj", [64, 128, 2048])

    MK = [k.sba(f"f1_mk{h}", [128, 1024], BF16) for h in range(4)]
    for h in range(4):
        k.dma("pool", MK[h][:, :], mk[h], writes=(("mk", h),))
    UPT = k.sba("f1_upt", [128, 2048], BF16)
    Y = [k.sba(f"f1_y{i}", [128, 512], BF16) for i in range(32)]
    HS = [k.sba(f"f1_hs{i}", [128, 512], BF16) for i in range(16)]

    for s in range(NSTRIPE):
        sl = slice(s * ST_T, (s + 1) * ST_T)
        for h in range(4):
            for c in range(NCHUNK):
                i = h * 8 + c
                pa, pak = k.pp[i % 2], k.ppk[i % 2]
                k.mm(pa[:, :], MK[h][:, c * 128:(c + 1) * 128], k.XBF[c][:, sl],
                     True, True, (("mk", h), ("xbf", c, s)), pak)
                k.v_copy(Y[i][:, :], pa[:, :], reads=(pak,), writes=(("y", i),))
        for grp in range(8):
            hh = grp // 2
            k.dma("pool", UPT[:, :], up[hh][:, (grp % 2) * 2048:(grp % 2) * 2048 + 2048],
                  writes=("f1_upt",))
            gtiles = []
            for l in range(4):
                p0 = grp * 8 + 2 * l
                wt, wk = k.wload(cp[p0:p0 + 2].transpose([1, 0, 2]), 4096, "cp")
                gtiles.append((wt, wk))
            for j in range(16):
                hc = grp * 16 + j
                rem = hc % 32
                cpr, msub = rem // 4, rem % 4
                i = hh * 8 + cpr
                upcol = (cpr % 4) * 512 + msub * 128
                pa, pak = k.pp[4 + j % 2], k.ppk[4 + j % 2]
                k.mm(pa[:, :], UPT[:, upcol: upcol + 128],
                     Y[i][:, :], True, True, ("f1_upt", ("y", i)), pak)
                k.actv(HS[j][:, :], pa[:, :], AF.Gelu, reads=(pak,), writes=(("hs", j),))
            for half in range(2):
                for j in range(16):
                    wt, wk = gtiles[j // 4]
                    jj = j % 4
                    base = (jj // 2) * 2048 + (jj % 2) * 1024
                    for ec in range(half * 4, half * 4 + 4):
                        pa, pak = k.pp[ec % 4], k.ppk[ec % 4]
                        k.mm(pa[:, :], wt[:, base + ec * 128: base + (ec + 1) * 128],
                             HS[j][:, :], j == 0, j == 15, (wk, ("hs", j)), pak)
                for ec in range(half * 4, half * 4 + 4):
                    pa, pak = k.pp[ec % 4], k.ppk[ec % 4]
                    k.v_tt(k.ST[ec][:, sl], k.ST[ec][:, sl], pa[:, :], ALU.add,
                           reads=(("st", ec, s), pak), writes=(("st", ec, s),))
        k.ln_e(s)


def emit_l1_attn(k: KB):
    qt = k.din("a1_q", [8, 128, 1024])
    kt = k.din("a1_k", [8, 128, 1024])
    vt = k.din("a1_v", [1024, 1024])
    pj = k.din("a1_proj", [8, 128, 1024])
    alb = k.din("a1_ab", [16, 2])
    ab = k.sba("a1_ab", [16, 2], F32)
    k.dma("sp", ab[:, :], alb, writes=("a1_ab",))
    scc = k.sba("a1_scc", [16, 2], F32)
    scr = k.sba("a1_scr", [1, 16], F32)
    tmp = k.sba("a1_tmp", [16, 2], F32)
    mx = k.sba("a1_mx", [16, 2], F32)
    one = k.tensors["c_one"]
    k.actv(tmp[:, :], ab[:, :], AF.Abs, reads=("a1_ab",), writes=("a1_tmp",))
    k.actv(tmp[:, :], tmp[:, :], AF.Exp, reads=("a1_tmp",), writes=("a1_tmp",), scale=-1.0)
    k.actv(tmp[:, :], tmp[:, :], AF.Ln, reads=("a1_tmp", "c_one"), writes=("a1_tmp",),
           bias=one[0:16, 0:1])
    k.v_ts(mx[:, :], ab[:, :], 0.0, None, ALU.max, None, reads=("a1_ab",), writes=("a1_mx",))
    for c2 in range(2):
        k.actv(scc[:, c2:c2 + 1], tmp[:, c2:c2 + 1], AF.Relu,
               reads=("a1_tmp", "a1_mx"), writes=("a1_scc",), bias=mx[:, c2:c2 + 1])
        k.v_ts(scc[:, c2:c2 + 1], scc[:, c2:c2 + 1], 1e-6, None, ALU.add, None,
               reads=("a1_scc",), writes=("a1_scc",))
    k.v_ts(scc[:, 0:1], scc[:, 0:1], scc[:, 1:2], 8.0, ALU.mult, ALU.mult,
           reads=("a1_scc",), writes=("a1_scc",))
    k.dbg("scc", scc[:, :], ("a1_scc",))
    scds = k.nc.dram_tensor("a1_scds", [1, 16], F32, kind="Internal")
    k.dma("pool", scds.ap(), scc[0:16, 0:1], reads=("a1_scc",), writes=("a1_scds",))
    scbc = k.sba("a1_scbc", [128, 16], F32)
    k.dma("pool", scbc[:, :], scds.ap().broadcast_to([128, 16]),
          reads=("a1_scds",), writes=("a1_scbc",))
    k.dbg("scbc", scbc[:, :], ("a1_scbc",))
    RCP = [k.sba(f"a1_rcp{h}", [128, 16], F32) for h in range(16)]
    pos = k.tensors["c_pos"]
    dena = k.sba("a1_dena", [128, 16], F32)
    denb = k.sba("a1_denb", [128, 16], F32)
    k.dbg("cpos", k.tensors["c_pos"][:, :], ("c_pos",))
    for h in range(16):
        k.v_ts(dena[:, :], pos[:, :], scbc[:, h:h + 1], None, ALU.mult, None,
               reads=("c_pos", "a1_scbc"), writes=("a1_dena",))
        k.v_ts(denb[:, :], dena[:, :], 1e-6, None, ALU.max, None,
               reads=("a1_dena",), writes=("a1_denb",))
        if h == 0:
            k.dbg("dena", dena[:, :], ("a1_dena",))
            k.dbg("denb", denb[:, :], ("a1_denb",))
        k.tk.op("dve", lambda e, o=RCP[h][:, :], i=denb[:, :]: e.reciprocal(o, i),
                reads=("a1_denb",), writes=(("rcp", h),), strict=True)

    SF = [k.sba(f"a1_sf{h}", [128, 64], F32) for h in range(16)]
    SB = [k.sba(f"a1_sb{h}", [128, 64], BF16) for h in range(16)]
    for h in range(16):
        k.memset(SF[h][:, :], 0.0, writes=(("sf", h),), eng="pool")
        k.memset(SB[h][:, :], 0.0, writes=(("sbp", h),), eng="pool")

    PHQ = [k.sba(f"a1_phq{c}", [128, 512], BF16) for c in range(NCHUNK)]
    PHK = [k.sba(f"a1_phk{c}", [128, 512], BF16) for c in range(NCHUNK)]
    VT = [k.sba(f"a1_vt{t}", [128, 1024], BF16) for t in range(CC_N)]
    OT = [k.sba(f"a1_ot{t}", [128, 1024], BF16) for t in range(CC_N)]
    KT2 = [k.sba(f"a1_kt2{i}", [128, 128], BF16) for i in range(4)]
    MS = [k.sba(f"a1_ms{i}", [128, 128], BF16) for i in range(8)]
    mask = k.tensors["c_mask"]

    for s in range(NSTRIPE):
        sl = slice(s * ST_T, (s + 1) * ST_T)
        xkeys = tuple(("xbf", c, s) for c in range(NCHUNK))
        for part, tiles, src in (("q", PHQ, qt), ("k", PHK, kt)):
            for oc in range(NCHUNK):
                wt, wk = k.wload(src[oc], 1024, "qk")
                pa, pak = k.pp[oc % 2], k.ppk[oc % 2]
                for kc in range(NCHUNK):
                    k.mm(pa[:, :], wt[:, kc * 128:(kc + 1) * 128], k.XBF[kc][:, sl],
                         kc == 0, kc == 7, (wk, ("xbf", kc, s)), pak)
                k.actv(tiles[oc][:, :], pa[:, :], AF.Relu, reads=(pak,),
                       writes=((part, oc),))
                k.v_ts(tiles[oc][:, :], tiles[oc][:, :], 1e-6, None, ALU.add, None,
                       reads=((part, oc),), writes=((part, oc),))
        vtiles = []
        for g4 in range(2):
            wt, wk = k.wload(vt[g4 * 512:(g4 + 1) * 512, :].rearrange("(a p) m -> a p m", p=128).transpose([1, 0, 2]),
                             4096, "v")
            vtiles.append((wt, wk))
        for half in range(2):
            hsl = slice(half * 512, half * 512 + 512)
            for kc in range(NCHUNK):
                wt, wk = vtiles[kc // 4]
                for tc in range(CC_N):
                    pa, pak = k.pp[tc], k.ppk[tc]
                    k.mm(pa[:, :], k.XBF[kc][:, s * ST_T + tc * 128: s * ST_T + (tc + 1) * 128],
                         wt[:, (kc % 4) * 1024 + half * 512:(kc % 4) * 1024 + half * 512 + 512],
                         kc == 0, kc == 7, (wk, ("xbf", kc, s)), pak)
            for tc in range(CC_N):
                k.v_copy(VT[tc][:, hsl], k.pp[tc][:, :], reads=(k.ppk[tc],), writes=(("vt", tc, half),))
        if s == 0:
            k.dbg("phq0", PHQ[0][:, :], (("q", 0),))
            k.dbg("phk0", PHK[0][:, :], (("k", 0),))
            k.dbg("vt0", VT[0][:, :], (("vt", 0, 0), ("vt", 0, 1)))
            k.dbg("rcp0", RCP[0][:, :], (("rcp", 0),))
        if k.paranoid >= 2:
            k.tk.barrier()
        for cc in range(CC_N):
            g = s * CC_N + cc
            csl = slice(s * ST_T + cc * 128, s * ST_T + (cc + 1) * 128)
            if k.paranoid >= 3:
                k.tk.barrier()
            for kc in range(NCHUNK):
                ptp, ptpk = k.pt[kc % 2], k.ptk[kc % 2]
                k.tr(ptp[:, 0:128], PHK[kc][:, cc * 128:(cc + 1) * 128],
                     (("k", kc),), ptpk)
                k.v_copy(KT2[kc % 4][:, :], ptp[:, 0:128], reads=(ptpk,),
                         writes=(("a1_kt2", kc % 4),))
                for half in range(2):
                    h = kc * 2 + half
                    hb = 64 * half
                    psc, psck = k.pp[4 + h % 2], k.ppk[4 + h % 2]
                    k.mm(psc[:, 0:128], PHK[kc][hb:hb + 64, cc * 128:(cc + 1) * 128],
                         PHQ[kc][hb:hb + 64, cc * 128:(cc + 1) * 128], True, True,
                         (("k", kc), ("q", kc)), psck)
                    k.v_tt(MS[h % 8][:, :], psc[:, 0:128], mask[:, :], ALU.mult,
                           reads=(psck, "c_mask"), writes=(("ms", h % 8),))
                for half in range(2):
                    h = kc * 2 + half
                    hb = 64 * half
                    po, pok = k.pp[h % 4], k.ppk[h % 4]
                    k.mm(po[:, 0:64], MS[h % 8][:, :], VT[cc][:, h * 64:(h + 1) * 64],
                         True, False, (("ms", h % 8), ("vt", cc, h // 8)), pok)
                    k.mm(po[:, 0:64], PHQ[kc][:, cc * 128:(cc + 1) * 128],
                         SB[h][:, :], False, True, (("q", kc), ("sbp", h)), pok)
                    k.v_ts(OT[cc][:, h * 64:(h + 1) * 64], po[:, 0:64], RCP[h][:, g:g + 1],
                           None, ALU.mult, None, reads=(pok, ("rcp", h)), writes=(("ot", cc, h),))
                    pd, pdk = k.pp[4 + h % 2], k.ppk[4 + h % 2]
                    k.mm(pd[hb:hb + 64, 128:192], KT2[kc % 4][:, hb:hb + 64],
                         VT[cc][:, h * 64:(h + 1) * 64], True, True,
                         (("a1_kt2", kc % 4), ("vt", cc, h // 8)), pdk,
                         tile_position=(0, hb) if hb else None)
                    k.v_tt(SF[h][hb:hb + 64, :], SF[h][hb:hb + 64, :], pd[hb:hb + 64, 128:192],
                           ALU.add, reads=(("sf", h), pdk), writes=(("sf", h),))
                    k.v_copy(SB[h][hb:hb + 64, :], SF[h][hb:hb + 64, :],
                             reads=(("sf", h),), writes=(("sbp", h),))
        if s == 0:
            for t4 in range(4):
                k.dbg(f"ot{t4}", OT[t4][:, :], tuple(("ot", t4, h) for h in range(16)))
            k.dbg("sb0", SB[0][:, :], (("sbp", 0),))
            k.dbg("sb1", SB[1][:, :], (("sbp", 1),))
        if k.paranoid >= 2:
            k.tk.barrier()
        MF = VT
        for cc in range(CC_N):
            for ec in range(NCHUNK):
                ptp, ptpk = k.pt[(cc * NCHUNK + ec) % 2], k.ptk[(cc * NCHUNK + ec) % 2]
                k.tr(ptp[:, 0:128], OT[cc][:, ec * 128:(ec + 1) * 128],
                     tuple(("ot", cc, h) for h in range(ec * 2, ec * 2 + 2)), ptpk)
                k.v_copy(MF[ec // 2][:, (ec % 2) * 512 + cc * 128:(ec % 2) * 512 + (cc + 1) * 128],
                         ptp[:, 0:128], reads=(ptpk,),
                         writes=(("mf", ec), ("vt", ec // 2, ec % 2)))
        for ec in range(NCHUNK):
            wt, wk = k.wload(pj[ec], 1024, "pj")
            pa, pak = k.pp[ec % 2], k.ppk[ec % 2]
            for kc in range(NCHUNK):
                k.mm(pa[:, :], wt[:, kc * 128:(kc + 1) * 128],
                     MF[kc // 2][:, (kc % 2) * 512:(kc % 2) * 512 + 512],
                     kc == 0, kc == 7, (wk, ("mf", kc)), pak)
            k.v_tt(k.ST[ec][:, sl], k.ST[ec][:, sl], pa[:, :], ALU.add,
                   reads=(("st", ec, s), pak), writes=(("st", ec, s),))
        k.ln_e(s)


def emit_l0_attn(k: KB):
    cw = k.din("a0_convw", [8, 128, 4])
    qkg = k.din("a0_qkg", [4, 128, 2048])
    wv = k.din("a0_wv", [1024, 1024])
    wg = k.din("a0_wg", [1024, 1024])
    wo = k.din("a0_wout", [8, 128, 1024])
    bg = k.din("a0_bg", [1, 1024])
    aug = k.din("a0_aug", [128, 8])

    CW = k.sba("a0_cw", [128, 32], F32)
    for c in range(NCHUNK):
        k.dma("sp", CW[:, c * 4:(c + 1) * 4], cw[c], writes=(("cw", c),))
    BGR = k.sba("a0_bgr", [1, 1024], BF16)
    k.dma("pool", BGR[:, :], bg, writes=("a0_bgr",))
    AUG = k.sba("a0_augt", [128, 8], BF16)
    k.dma("pool", AUG[:, :], aug, writes=("a0_augt",))

    SF2 = [k.sba(f"a0_sf{kc}", [128, 128], F32) for kc in range(4)]
    SBp = [k.sba(f"a0_sb{h}", [128, 128], BF16) for h in range(8)]
    for kc in range(4):
        k.memset(SF2[kc][:, :], 0.0, writes=(("a0sf", kc),), eng="pool")
    for h in range(8):
        k.memset(SBp[h][:, :], 0.0, writes=(("a0sb", h),), eng="pool")

    KGF = k.sba("a0_kgf", [128, 512], F32)
    SC1 = k.sba("a0_sc1", [128, 512], F32)
    GK = [k.sba("a0_gk0", [128, 512], F32)]
    G = [k.sba("a0_g0", [128, 512], F32)]
    KF = [k.sba("a0_kf0", [128, 512], F32)]
    EB = [k.sba("a0_eb0", [128, 512], F32)]
    QT = [k.sba(f"a0_qt{c}", [128, 512], BF16) for c in range(4)]
    KT = [k.sba(f"a0_kt{c}", [128, 512], BF16) for c in range(4)]
    KH = [k.sba(f"a0_kh{c}", [128, 512], BF16) for c in range(4)]
    U = [k.sba(f"a0_u{c}", [128, 512], BF16) for c in range(4)]
    BT = k.sba("a0_bt", [128, 16], F32)
    BCOL = k.sba("a0_bcol", [128, 4], F32)
    EH = KGF
    VT = [k.sba(f"a0_vt{t}", [128, 1024], BF16) for t in range(CC_N)]
    SILG = [k.sba(f"a0_sg{t}", [128, 1024], BF16) for t in range(CC_N)]
    OT = [k.sba(f"a0_ot{t}", [128, 1024], F32) for t in range(CC_N)]
    KHT = [k.sba(f"a0_kht{i}", [128, 128], BF16) for i in range(4)]
    MS = [k.sba(f"a0_ms{i}", [128, 128], BF16) for i in range(8)]
    SGP = [k.sba(f"a0_sgp{i}", [128, 256], F32) for i in range(2)]
    AW = k.sba("a0_aw", [128, 8], F32)
    DCOL = [k.sba(f"a0_dcol{i}", [128, 128], F32) for i in range(2)]
    MCOL = [k.sba(f"a0_mcol{i}", [128, 24], F32) for i in range(2)]
    VCOL = [k.sba(f"a0_vcol{i}", [128, 16], F32) for i in range(2)]
    SQD = DCOL
    XTAIL = k.sba("a0_xt", [128, 32], F32)
    mask = k.tensors["c_mask"]
    Z = k.tensors["c_zero"]

    def conv_stripe(s):
        sl = slice(s * ST_T, (s + 1) * ST_T)
        for c in range(NCHUNK):
            acc = SC1
            k.v_ts(acc[:, :], k.ST[c][:, sl], CW[:, c * 4 + 3:c * 4 + 4], None, ALU.mult, None,
                   reads=(("st", c, s), ("cw", c)), writes=("a0_sc1",))
            for jj in (2, 1, 0):
                off = 3 - jj
                k.v_stt(acc[:, off:512], k.ST[c][:, s * ST_T: s * ST_T + 512 - off],
                        CW[:, c * 4 + jj:c * 4 + jj + 1], acc[:, off:512], ALU.mult, ALU.add,
                        reads=(("st", c, s), ("cw", c), "a0_sc1"), writes=("a0_sc1",))
                if s > 0:
                    k.v_stt(acc[:, 0:off], XTAIL[:, c * 4 + 4 - off:c * 4 + 4],
                            CW[:, c * 4 + jj:c * 4 + jj + 1], acc[:, 0:off], ALU.mult, ALU.add,
                            reads=(("a0xt", c), ("cw", c), "a0_sc1"), writes=("a0_sc1",))
            k.actv(k.XBF[c][:, sl], acc[:, :], AF.Silu, reads=("a0_sc1",), writes=(("xbf", c, s),))

    for s in range(NSTRIPE):
        sl = slice(s * ST_T, (s + 1) * ST_T)
        conv_stripe(s)
        if s < NSTRIPE - 1:
            for c in range(NCHUNK):
                k.v_copy(XTAIL[:, c * 4:c * 4 + 4],
                         k.ST[c][:, (s + 1) * ST_T - 4:(s + 1) * ST_T],
                         reads=(("st", c, s),), writes=(("a0xt", c),))
        for kc in range(4):
            wt, wk = k.wload(qkg[kc], 2048, "qkg")
            par = kc % 2
            pa, pak = k.pp[par], k.ppk[par]
            for c in range(NCHUNK):
                k.mm(pa[:, :], wt[:, 1024 + c * 128:1024 + (c + 1) * 128], k.XBF[c][:, sl],
                     c == 0, c == 7, (wk, ("xbf", c, s)), pak)
            gk = GK[0]
            k.v_copy(KGF[:, :], pa[:, :], reads=(pak,), writes=("a0_kgf",))
            k.v_ts(gk[:, :], KGF[:, :], 0.0, None, ALU.min, None, reads=("a0_kgf",), writes=("a0gk",))
            k.actv(SC1[:, :], KGF[:, :], AF.Abs, reads=("a0_kgf",), writes=("a0_sc1",))
            k.actv(SC1[:, :], SC1[:, :], AF.Exp, reads=("a0_sc1",), writes=("a0_sc1",), scale=-1.0)
            k.actv(SC1[:, :], SC1[:, :], AF.Ln, reads=("a0_sc1", "c_one"), writes=("a0_sc1",),
                   bias=k.tensors["c_one"][:, 0:1])
            k.v_stt(gk[:, :], SC1[:, :], -1.0, gk[:, :], ALU.mult, ALU.add,
                    reads=("a0_sc1", "a0gk"), writes=("a0gk",))
            g_ = G[0]
            for cc in range(CC_N):
                k.v_scan(g_[:, cc * 128:(cc + 1) * 128], gk[:, cc * 128:(cc + 1) * 128],
                         Z[:, :], 0.0, ALU.add, ALU.add,
                         reads=("a0gk", "c_zero"), writes=("a0g",))
            if s == 0 and kc == 0 and k.debug:
                k.dbg("agk", gk[:, :], ("a0gk",))
                k.dbg("ag", g_[:, :], ("a0g",))
            kf = KF[0]
            k.actv(SC1[:, :], gk[:, :], AF.Exp, reads=("a0gk",), writes=("a0_sc1",),
                   scale=1.0 / 16)
            k.v_ts(kf[:, :], SC1[:, :], -1.0, 1.0, ALU.mult, ALU.add,
                   reads=("a0_sc1",), writes=("a0kf",))
            eb = EB[0]
            k.actv(eb[:, :], g_[:, :], AF.Exp, reads=("a0g",), writes=("a0eb",),
                   scale=1.0 / 16)
            k.actv(KGF[:, :], g_[:, :], AF.Exp, reads=("a0g",), writes=("a0_kgf",),
                   scale=-1.0 / 16)
            k.v_tt(KT[kc][:, :], kf[:, :], KGF[:, :], ALU.mult,
                   reads=("a0kf", "a0_kgf"), writes=(("a0kt", kc),))
            k.actv(BT[:, kc * 4:(kc + 1) * 4], g_[:, 127::128], AF.Exp,
                   reads=("a0g",), writes=("a0_bt",), scale=1.0 / 16)
            k.v_ts(BCOL[:, 0:4], g_[:, 127::128], 1.0 / 16, None, ALU.mult, None,
                   reads=("a0g",), writes=("a0_bcol",))
            for cc in range(CC_N):
                k.actv(EH[:, 0:128], g_[:, cc * 128:(cc + 1) * 128], AF.Exp,
                       reads=("a0g", "a0_bcol"), writes=("a0_eh", "a0_kgf"),
                       scale=-1.0 / 16, bias=BCOL[:, cc:cc + 1])
                k.v_tt(KH[kc][:, cc * 128:(cc + 1) * 128], kf[:, cc * 128:(cc + 1) * 128],
                       EH[:, 0:128], ALU.mult, reads=("a0kf", "a0_eh"), writes=(("a0kh", kc),))
            pb, pbk = k.pp[2 + par], k.ppk[2 + par]
            for c in range(NCHUNK):
                k.mm(pb[:, :], wt[:, c * 128:(c + 1) * 128], k.XBF[c][:, sl],
                     c == 0, c == 7, (wk, ("xbf", c, s)), pbk)
            k.v_stt(QT[kc][:, :], pb[:, :], 0.125, eb[:, :], ALU.mult, ALU.mult,
                    reads=(pbk, "a0eb"), writes=(("a0qt", kc),))
            k.v_tt(U[kc][:, :], pb[:, :], kf[:, :], ALU.mult,
                   reads=(pbk, "a0kf"), writes=(("a0u", kc),))
        onesbf = k.tensors["c_onesbf"]
        for dst, wsrc, tag, with_bias in ((VT, wv, "a0vt", False), (SILG, wg, "a0sg", True)):
            tiles = []
            for g4 in range(2):
                wt, wk = k.wload(wsrc[g4 * 512:(g4 + 1) * 512, :]
                                 .rearrange("(a p) m -> a p m", p=128).transpose([1, 0, 2]),
                                 4096, "wvg")
                tiles.append((wt, wk))
            for half in range(2):
                hsl = slice(half * 512, half * 512 + 512)
                for kc in range(NCHUNK):
                    wt, wk = tiles[kc // 4]
                    for tc in range(CC_N):
                        pa, pak = k.pp[tc], k.ppk[tc]
                        k.mm(pa[:, :],
                             k.XBF[kc][:, s * ST_T + tc * 128: s * ST_T + (tc + 1) * 128],
                             wt[:, (kc % 4) * 1024 + half * 512:(kc % 4) * 1024 + half * 512 + 512],
                             kc == 0, (kc == 7 and not with_bias), (wk, ("xbf", kc, s)), pak)
                if with_bias:
                    for tc in range(CC_N):
                        pa, pak = k.pp[tc], k.ppk[tc]
                        k.mm(pa[:, :], onesbf[0:1, :], BGR[0:1, half * 512:half * 512 + 512],
                             False, True, ("c_onesbf", "a0_bgr"), pak)
                for tc in range(CC_N):
                    if with_bias:
                        k.actv(SC1[:, :], k.pp[tc][:, :], AF.Sigmoid,
                               reads=(k.ppk[tc],), writes=("a0_sc1",))
                        k.v_tt(SILG[tc][:, hsl], SC1[:, :], k.pp[tc][:, :], ALU.mult,
                               reads=("a0_sc1", k.ppk[tc]), writes=((tag, tc, half),))
                    else:
                        k.v_copy(dst[tc][:, hsl], k.pp[tc][:, :], reads=(k.ppk[tc],),
                                 writes=((tag, tc, half),), eng="act")
        for cc in range(CC_N):
            for kc in range(4):
                ptp, ptpk = k.pt[kc % 2], k.ptk[kc % 2]
                k.tr(ptp[:, 0:128], KH[kc][:, cc * 128:(cc + 1) * 128],
                     (("a0kh", kc),), ptpk)
                k.v_copy(KHT[kc][:, :], ptp[:, 0:128], reads=(ptpk,), writes=(("a0kht", kc),))
                paw, pawk = k.pp[2], k.ppk[2]
                k.mm(paw[:, 256 + kc * 2:258 + kc * 2], U[kc][:, cc * 128:(cc + 1) * 128],
                     AUG[:, kc * 2:kc * 2 + 2], True, True, (("a0u", kc), "a0_augt"), pawk)
            k.v_copy(AW[:, :], k.pp[2][:, 256:264], reads=(k.ppk[2],), writes=("a0_aw",))
            for h in range(8):
                kc, half = h // 2, h % 2
                hb = 64 * half
                psc, psck = k.pp[4 + h % 2], k.ppk[4 + h % 2]
                k.mm(psc[:, 0:128], KT[kc][hb:hb + 64, cc * 128:(cc + 1) * 128],
                     QT[kc][hb:hb + 64, cc * 128:(cc + 1) * 128], True, True,
                     ((("a0kt", kc)), ("a0qt", kc)), psck)
                k.v_tt(MS[h][:, :], psc[:, 0:128], mask[:, :], ALU.mult,
                       reads=(psck, "c_mask"), writes=(("a0ms", h),))
            for h in range(8):
                kc, half = h // 2, h % 2
                prot = kc % 2
                if half == 0:
                    sg = SGP[prot]
                    for hh in (h, h + 1):
                        k.v_ts(sg[:, (hh % 2) * 128:(hh % 2) * 128 + 128],
                               VT[cc][:, hh * 128:(hh + 1) * 128],
                               AW[:, kc * 2 + hh % 2:kc * 2 + hh % 2 + 1], None, ALU.mult, None,
                               reads=((("a0vt", cc, hh // 4)), "a0_aw"), writes=(("a0_sgp", prot),))
                    k.actv(sg[:, :], sg[:, :], AF.Sigmoid, reads=(("a0_sgp", prot),),
                           writes=(("a0_sgp", prot),))
                po, pok = k.pp[h % 4], k.ppk[h % 4]
                k.mm(po[:, 0:128], MS[h][:, :], VT[cc][:, h * 128:(h + 1) * 128], True, False,
                     (("a0ms", h), ("a0vt", cc, h // 4)), pok)
                k.mm(po[:, 0:128], QT[kc][:, cc * 128:(cc + 1) * 128], SBp[h][:, :],
                     False, True, ((("a0qt", kc)), ("a0sb", h)), pok)
                k.v_tt(OT[cc][:, h * 128:(h + 1) * 128], po[:, 0:128],
                       SGP[prot][:, half * 128:half * 128 + 128], ALU.add,
                       reads=(pok, ("a0_sgp", prot)), writes=(("a0ot", cc, h),))
            for h in range(8):
                kc, half = h // 2, h % 2
                hb = 64 * half
                pd, pdk = k.pp[2 + h % 2], k.ppk[2 + h % 2]
                k.mm(pd[hb:hb + 64, 0:128], KHT[kc][:, hb:hb + 64],
                     VT[cc][:, h * 128:(h + 1) * 128], True, True,
                     (("a0kht", kc), ("a0vt", cc, h // 4)), pdk,
                     tile_position=(0, hb) if hb else None)
                k.v_stt(SF2[kc][hb:hb + 64, :], SF2[kc][hb:hb + 64, :],
                        BT[hb:hb + 64, kc * 4 + cc:kc * 4 + cc + 1], pd[hb:hb + 64, 0:128],
                        ALU.mult, ALU.add, reads=(("a0sf", kc), "a0_bt", pdk), writes=(("a0sf", kc),))
                k.v_copy(SBp[h][hb:hb + 64, :], SF2[kc][hb:hb + 64, :],
                         reads=(("a0sf", kc),), writes=(("a0sb", h),))
        for cc in range(CC_N):
            rot = cc % 2
            mcol, vcol = MCOL[rot], VCOL[rot]
            for h in range(8):
                osl = OT[cc][:, h * 128:(h + 1) * 128]
                k.v_red(mcol[:, h:h + 1], osl, reads=(("a0ot", cc, h),),
                        writes=(("a0_mcol", rot),))
                k.actv(DCOL[h % 2][:, :], osl, AF.Square, reads=(("a0ot", cc, h),),
                       writes=(("a0_sqd", rot), ("a0_dcol", h % 2)),
                       accum_out=vcol[:, h:h + 1])
            k.v_ts(mcol[:, 8:16], mcol[:, 0:8], 1.0 / 128, None, ALU.mult, None,
                   reads=(("a0_mcol", rot),), writes=(("a0_mcol", rot),))
            k.v_ts(mcol[:, 16:24], mcol[:, 8:16], None, None, ALU.mult, None,
                   reads=(("a0_mcol", rot),), writes=(("a0_mcol", rot),), sq=True)
            k.v_stt(vcol[:, 8:16], vcol[:, 0:8], 1.0 / 128, mcol[:, 16:24],
                    ALU.mult, ALU.subtract,
                    reads=(("a0_mcol", rot), ("a0_sqd", rot)), writes=(("a0_vcol", rot),))
            k.tk.op("act", lambda e, o=vcol[:, 8:16], i=vcol[:, 8:16],
                    b=k.tensors["c_eps5"][:, 0:1]:
                    e.activation(o, i, AF.Ln, bias=b, scale=1.0),
                    reads=(("a0_vcol", rot), "c_eps5"), writes=(("a0_vcol", rot),))
            k.actv(vcol[:, 8:16], vcol[:, 8:16], AF.Exp, reads=(("a0_vcol", rot),),
                   writes=(("a0_vcol", rot),), scale=-0.5)
            for h in range(8):
                osl = OT[cc][:, h * 128:(h + 1) * 128]
                dcol = DCOL[h % 2]
                k.v_ts(dcol[:, :], osl, mcol[:, 8 + h:9 + h], vcol[:, 8 + h:9 + h],
                       ALU.subtract, ALU.mult,
                       reads=(("a0ot", cc, h), ("a0_mcol", rot), ("a0_vcol", rot)),
                       writes=(("a0_dcol", h % 2),))
                k.v_tt(SILG[cc][:, h * 128:(h + 1) * 128], dcol[:, :],
                       SILG[cc][:, h * 128:(h + 1) * 128], ALU.mult,
                       reads=(("a0_dcol", h % 2), ("a0sg", cc, h // 4)),
                       writes=(("a0m", cc, h),))
        if s == 0 and k.debug:
            for c4 in range(2):
                k.dbg(f"xc{c4}", k.XBF[c4][:, sl], (("xbf", c4, 0),))
            k.dbg("qt0", QT[0][:, :], (("a0qt", 0),))
            k.dbg("kt0", KT[0][:, :], (("a0kt", 0),))
            k.dbg("kh0", KH[0][:, :], (("a0kh", 0),))
            k.dbg("u0", U[0][:, :], (("a0u", 0),))
            k.dbg("bt", BT[:, :], ("a0_bt",))
            k.dbg("sg0", SILG[0][:, :], (("a0sg", 0, 0), ("a0sg", 0, 1)))
            for t4 in range(4):
                k.dbg(f"aot{t4}", OT[t4][:, :], tuple(("a0ot", t4, h) for h in range(8)))
            k.dbg("asf0", SF2[0][:, :], (("a0sf", 0),))
        MF = VT
        for cc in range(CC_N):
            for ec in range(NCHUNK):
                ptp, ptpk = k.pt[(cc * NCHUNK + ec) % 2], k.ptk[(cc * NCHUNK + ec) % 2]
                k.tr(ptp[:, 0:128], SILG[cc][:, ec * 128:(ec + 1) * 128],
                     tuple(("a0m", cc, h) for h in (ec,)), ptpk)
                k.v_copy(MF[ec // 2][:, (ec % 2) * 512 + cc * 128:(ec % 2) * 512 + (cc + 1) * 128],
                         ptp[:, 0:128], reads=(ptpk,),
                         writes=(("a0mf", ec), ("a0vt", ec // 2, ec % 2)))
        for ec in range(NCHUNK):
            wt, wk = k.wload(wo[ec], 1024, "wo")
            pa, pak = k.pp[ec % 2], k.ppk[ec % 2]
            for kc in range(NCHUNK):
                k.mm(pa[:, :], wt[:, kc * 128:(kc + 1) * 128],
                     MF[kc // 2][:, (kc % 2) * 512:(kc % 2) * 512 + 512],
                     kc == 0, kc == 7, (wk, ("a0mf", kc)), pak)
            k.v_tt(k.ST[ec][:, sl], k.ST[ec][:, sl], pa[:, :], ALU.add,
                   reads=(("st", ec, s), pak), writes=(("st", ec, s),))
        k.ln_e(s)
        if k.paranoid:
            k.tk.barrier()


_BUILD_CACHE = {}


def build_program(blocks):
    blocks = tuple(blocks)
    if blocks in _BUILD_CACHE:
        return _BUILD_CACHE[blocks]
    k = KB(blocks)
    e5 = k.sb("c_eps5", [128, 1], F32)
    k.memset(e5[:, :], 1e-5, writes=("c_eps5",))
    k.setup_common()
    k.init_arena()
    emitters = {0: emit_l0_attn, 1: emit_l0_ffn, 2: emit_l1_attn, 3: emit_l1_ffn}
    for i, b in enumerate(blocks):
        emitters[b](k)
        if i < len(blocks) - 1:
            k.arena_reset()
    k.store_out()
    pass
    k.tk.finalize()
    _BUILD_CACHE[blocks] = k
    return k


def _tile_oc(wT, n_oc, n_kc):
    return np.ascontiguousarray(
        wT.reshape(n_kc, 128, n_oc, 128).transpose(2, 1, 0, 3).reshape(n_oc, 128, n_kc * 128))


def prep_weights(inp, blocks):
    f32 = np.float32
    d = {}
    if 0 in blocks:
        d["a0_convw"] = np.ascontiguousarray(inp["L0_attn_conv_w"].reshape(8, 128, 4))
        wq = np.asarray(inp["L0_attn_wq"], f32)
        wkg = np.asarray(inp["L0_attn_wkg"], f32)
        qt = _tile_oc(wq, 4, 8)
        kgt = _tile_oc(wkg, 4, 8)
        d["a0_qkg"] = np.ascontiguousarray(np.concatenate([qt, kgt], axis=2))
        d["a0_wv"] = np.ascontiguousarray(np.asarray(inp["L0_attn_wv"], f32))
        d["a0_wg"] = np.ascontiguousarray(np.asarray(inp["L0_attn_wg"], f32))
        d["a0_wout"] = _tile_oc(np.asarray(inp["L0_attn_wout"], f32), 8, 8)
        d["a0_bg"] = np.ascontiguousarray(np.asarray(inp["L0_attn_bg"], f32).reshape(1, 1024))
        aug = np.zeros((128, 8), f32)
        a8 = np.asarray(inp["L0_attn_aug_balance"], f32).reshape(8, 64)
        for kc in range(4):
            aug[0:64, 2 * kc] = a8[2 * kc]
            aug[64:128, 2 * kc + 1] = a8[2 * kc + 1]
        d["a0_aug"] = aug
    if 1 in blocks:
        w1t = _tile_oc(np.asarray(inp["L0_ffn_w1_w"], f32).T, 32, 8)
        w2t = _tile_oc(np.asarray(inp["L0_ffn_w2_w"], f32).T, 32, 8)
        d["f0_w12"] = np.ascontiguousarray(np.concatenate([w1t, w2t], axis=2))
        d["f0_h1"] = _tile_oc(np.asarray(inp["L0_ffn_h1_w"], f32).T, 32, 32)
        d["f0_h2"] = _tile_oc(np.asarray(inp["L0_ffn_h2_w"], f32).T, 32, 32)
        d["f0_w3"] = _tile_oc(np.asarray(inp["L0_ffn_w3_w"], f32).T, 8, 32)
        s1 = np.asarray(inp["L0_ffn_w1_s"], f32).reshape(32, 128).T
        s2 = np.asarray(inp["L0_ffn_w2_s"], f32).reshape(32, 128).T
        d["f0_s12"] = np.ascontiguousarray(np.concatenate([s1, s2], axis=1))
        h1s = np.asarray(inp["L0_ffn_h1_s"], f32).reshape(32, 128).T
        h2s = np.asarray(inp["L0_ffn_h2_s"], f32).reshape(32, 128).T
        d["f0_sh"] = np.ascontiguousarray(np.concatenate([h1s, h2s], axis=1))
        d["f0_s3"] = np.ascontiguousarray(np.asarray(inp["L0_ffn_w3_s"], f32).reshape(8, 128).T)
    if 2 in blocks:
        qkv = np.asarray(inp["L1_attn_qkv_w"], f32)
        d["a1_q"] = _tile_oc(qkv[0:1024].T, 8, 8)
        d["a1_k"] = _tile_oc(qkv[1024:2048].T, 8, 8)
        d["a1_v"] = np.ascontiguousarray(qkv[2048:3072].T)
        d["a1_proj"] = _tile_oc(np.asarray(inp["L1_attn_proj_w"], f32).T, 8, 8)
        d["a1_ab"] = np.ascontiguousarray(np.stack(
            [np.asarray(inp["L1_attn_alpha_log"], f32),
             np.asarray(inp["L1_attn_beta_log"], f32)], axis=1))
    if 3 in blocks:
        rm = np.asarray(inp["L1_ffn_rail_maker"], f32).reshape(4, 8, 8, 16, 16)
        mkb = np.zeros((4, 8, 128, 128), f32)
        for rl in range(8):
            mkb[:, :, rl * 16:(rl + 1) * 16, rl * 16:(rl + 1) * 16] = rm[:, :, rl]
        d["f1_mk"] = np.ascontiguousarray(mkb.transpose(0, 2, 1, 3).reshape(4, 128, 1024))
        ru = np.asarray(inp["L1_ffn_rail_up_proj"], f32).reshape(4, 8, 8, 64, 16)
        upb = np.zeros((4, 8, 128, 512), f32)
        for rl in range(8):
            upb[:, :, rl * 16:(rl + 1) * 16, rl * 64:(rl + 1) * 64] = \
                np.swapaxes(ru[:, :, rl], -1, -2)
        d["f1_up"] = np.ascontiguousarray(upb.transpose(0, 2, 1, 3).reshape(4, 128, 4096))
        cpt = np.asarray(inp["L1_ffn_w_cproj"], f32).T
        d["f1_cproj"] = np.ascontiguousarray(
            cpt.reshape(64, 2, 128, 1024).transpose(0, 2, 1, 3).reshape(64, 128, 2048))
    d["c_idbf"] = np.eye(128, dtype=ml_dtypes.bfloat16)
    d["c_mask"] = np.triu(np.ones((128, 128), ml_dtypes.bfloat16))
    d["c_ones"] = np.full((128, 128), 1.0 / 1024, ml_dtypes.bfloat16)
    d["c_onesrow"] = np.ones((1, 128), f32)
    d["c_onesbf"] = np.ones((1, 128), ml_dtypes.bfloat16)
    d["c_pos"] = (np.arange(16, dtype=f32)[None, :] * 128
                  + np.arange(128, dtype=f32)[:, None] + 1.0)
    return d


def kernel(**inputs):
    x = np.asarray(inputs["L0_attn_x"], np.float32)
    xT = [np.ascontiguousarray(x[b].T) for b in range(B)]
    for grp in GROUPS:
        kb = build_program(grp)
        wd = prep_weights(inputs, grp)
        in_maps = []
        for b in range(B):
            m = {"x_in": xT[b]}
            for n in kb.in_names:
                if n != "x_in":
                    m[n] = wd[n]
            in_maps.append(m)
        res = run_bass_kernel_spmd(kb.nc, in_maps, core_ids=list(range(B)))
        xT = [np.asarray(res.results[b]["x_out"], np.float32) for b in range(B)]
    out = np.stack([xTb.T for xTb in xT], axis=0)
    return np.ascontiguousarray(out)
